# version 07
# baseline (speedup 1.0000x reference)
import math
import os
import resource
from contextlib import ExitStack

os.environ.setdefault("GAUGE_NTFF_IGNORE_DMA", "1")

import numpy as np

try:
    resource.setrlimit(resource.RLIMIT_STACK,
                       (resource.RLIM_INFINITY, resource.RLIM_INFINITY))
except (ValueError, OSError):
    try:
        resource.setrlimit(resource.RLIMIT_STACK,
                           (512 * 1024 * 1024, resource.RLIM_INFINITY))
    except (ValueError, OSError):
        pass
import ml_dtypes
import concourse.bass as bass
import concourse.mybir as mybir
from concourse.bass_utils import run_bass_kernel_spmd

F32 = mybir.dt.float32
BF16 = mybir.dt.bfloat16
AF = mybir.ActivationFunctionType
ALU = mybir.AluOpType
AX = mybir.AxisListType

NCORES = 8
B_PER_CORE = 2
S = 2048
D = 1024
T = B_PER_CORE * S
TT = T // 128
DG = D // 128
EPS = 1e-5
SQ2I = 1.0 / math.sqrt(2.0)


class Sem:
    def __init__(self, stack, nc, name):
        self.h = stack.enter_context(nc.semaphore(name))
        self.n = 0

    def inc(self, instr, k=1):
        instr.then_inc(self.h, k)
        self.n += k
        return self.n


class P:
    def __init__(self, nc, block, stack):
        self.nc = nc
        self.block = block
        self.gstack = stack
        self.pstack = None
        self.ld = Sem(stack, nc, "ld")
        self.ldg = Sem(stack, nc, "ldg")
        self.lds = Sem(stack, nc, "lds")
        self.st = Sem(stack, nc, "st")
        self.tC = Sem(stack, nc, "tC")
        self.vC = Sem(stack, nc, "vC")
        self.sC = Sem(stack, nc, "sC")
        self.gC = Sem(stack, nc, "gC")
        self.ph = Sem(stack, nc, "ph")
        self.r = [Sem(stack, nc, f"r{j}") for j in range(24)]
        self.phase_idx = 0
        self.consts = {}
        self.scratch = {}
        self._uid = 0

    def uid(self):
        self._uid += 1
        return self._uid

    def sbuf(self, name, shape, dtype, persistent=False):
        st = self.gstack if persistent else self.pstack
        return st.enter_context(
            self.nc.sbuf_tensor(f"{name}_{self.uid()}", shape, dtype))

    def psum(self, name, shape, dtype=F32):
        return self.pstack.enter_context(
            self.nc.psum_tensor(f"{name}_{self.uid()}", shape, dtype))

    def load_serial(self, e, sem, out, in_):
        e.wait_ge(sem.h, sem.n)
        sem.inc(e.dma_start(out=out, in_=in_), 16)


def phase_begin(p):
    assert p.pstack is None
    p.pstack = ExitStack()


def phase_end(p):
    p.phase_idx += 1
    target = 5 * p.phase_idx
    for eng in ("sync", "tensor", "vector", "scalar", "gpsimd"):
        @getattr(p.block, eng)
        def _(e):
            e.sem_inc(p.ph.h, 1)
            e.wait_ge(p.ph.h, target)
    p.ph.n = target
    p.pstack.close()
    p.pstack = None


def emit_consts(p):
    idb = p.sbuf("c_id_bf", [128, 128], BF16, persistent=True)
    idf = p.sbuf("c_id_f32", [128, 128], F32, persistent=True)
    ones_f = p.sbuf("c_ones_f", [1, 512], F32, persistent=True)
    ones_b = p.sbuf("c_ones_b", [1, 512], BF16, persistent=True)
    ones_colb = p.sbuf("c_ones_colb", [128, 1], BF16, persistent=True)
    p.consts.update(id_bf=idb, id_f32=idf, ones_f=ones_f, ones_b=ones_b,
                    ones_colb=ones_colb)

    @p.block.gpsimd
    def _(g):
        p.gC.inc(g.memset(idf[:], 1.0))
        g.wait_ge(p.gC.h, p.gC.n)
        g.affine_select(out=idf[:], in_=idf[:], pattern=[[1, 128]],
                        compare_op=ALU.is_equal, fill=0.0, base=0,
                        channel_multiplier=-1)
        g.memset(ones_f[:], 1.0)
        g.memset(ones_colb[:], 1.0)
        p.gC.inc(g.memset(ones_b[:], 1.0))

    @p.block.vector
    def _(v):
        v.wait_ge(p.gC.h, p.gC.n)
        p.vC.inc(v.tensor_copy(out=idb[:], in_=idf[:]))

    for eng in ("tensor", "scalar", "sync", "gpsimd"):
        @getattr(p.block, eng)
        def _(e):
            e.wait_ge(p.vC.h, p.vC.n)



def load_bias_cols(p, name, src_ap, n):
    nt = n // 128
    if src_ap.ndim == 1:
        src_ap = src_ap.unsqueeze(0)
    cols = p.sbuf(name + "_cols", [128, nt], F32)
    lstack = ExitStack()
    row = lstack.enter_context(p.nc.sbuf_tensor(
        f"{name}_row_{p.uid()}", [1, n], F32))
    pt = lstack.enter_context(p.nc.psum_tensor(
        f"{name}_pt_{p.uid()}", [128, max(nt, 2)], F32))

    @p.block.sync
    def _(e):
        p.load_serial(e, p.ld, row[:], src_ap)

    @p.block.tensor
    def _(t):
        t.wait_ge(p.ld.h, p.ld.n)
        ins = None
        for j in range(nt):
            ins = t.transpose(pt[:, j:j + 1], row[:, j * 128:(j + 1) * 128],
                              p.consts["id_f32"][0:1, 0:1])
        p.tC.inc(ins)

    @p.block.vector
    def _(v):
        v.wait_ge(p.tC.h, p.tC.n)
        p.vC.inc(v.tensor_copy(out=cols[:], in_=pt[:, 0:nt]))

    for eng in ("tensor", "vector", "scalar", "gpsimd", "sync"):
        @getattr(p.block, eng)
        def _(e):
            e.wait_ge(p.tC.h, p.tC.n)
            e.wait_ge(p.vC.h, p.vC.n)
    lstack.close()
    return cols


def load_row(p, name, src_ap, n, dtype=F32):
    row = p.sbuf(name, [1, n], dtype)

    @p.block.sync
    def _(e):
        p.load_serial(e, p.ld, row[:], src_ap)
    return row


def load_bcast(p, name, src_ap, n, dtype=F32, scale=None, nparts=128,
               stack=None):
    if src_ap.ndim == 1:
        src_ap = src_ap.unsqueeze(0)
    if stack is None:
        out = p.sbuf(name, [nparts, n], dtype)
    else:
        out = stack.enter_context(p.nc.sbuf_tensor(
            f"{name}_{p.uid()}", [nparts, n], dtype))
    lstack = ExitStack()
    row = lstack.enter_context(p.nc.sbuf_tensor(
        f"{name}_row_{p.uid()}", [1, n], F32))
    pb = lstack.enter_context(p.nc.psum_tensor(
        f"{name}_pb_{p.uid()}", [nparts, n], F32))

    @p.block.sync
    def _(e):
        p.load_serial(e, p.ld, row[:], src_ap)

    @p.block.tensor
    def _(t):
        t.wait_ge(p.ld.h, p.ld.n)
        ins = None
        for j0 in range(0, n, 512):
            w = min(512, n - j0)
            ins = t.matmul(pb[:, j0:j0 + w], p.consts["ones_f"][0:1, 0:nparts],
                           row[:, j0:j0 + w], start=True, stop=True)
        p.tC.inc(ins)

    @p.block.vector
    def _(v):
        v.wait_ge(p.tC.h, p.tC.n)
        if scale is None:
            ins = v.tensor_copy(out=out[:], in_=pb[:])
        else:
            ins = v.tensor_scalar_mul(out=out[:], in0=pb[:], scalar1=float(scale))
        p.vC.inc(ins)

    for eng in ("tensor", "vector", "scalar", "gpsimd", "sync"):
        @getattr(p.block, eng)
        def _(e):
            e.wait_ge(p.vC.h, p.vC.n)
    lstack.close()
    return out


def bcast_row(p, name, src_row_ap, n, dtype=F32, scale=None, nparts=128):
    out = p.sbuf(name, [nparts, n], dtype)
    lstack = ExitStack()
    pb = lstack.enter_context(p.nc.psum_tensor(
        f"{name}_pb_{p.uid()}", [nparts, n], F32))

    @p.block.tensor
    def _(t):
        t.wait_ge(p.ld.h, p.ld.n)
        t.wait_ge(p.vC.h, p.vC.n)
        t.wait_ge(p.sC.h, p.sC.n)
        ins = None
        for j0 in range(0, n, 512):
            w = min(512, n - j0)
            ins = t.matmul(pb[:, j0:j0 + w], p.consts["ones_f"][0:1, 0:nparts],
                           src_row_ap[:, j0:j0 + w], start=True, stop=True)
        p.tC.inc(ins)

    @p.block.vector
    def _(v):
        v.wait_ge(p.tC.h, p.tC.n)
        if scale is None:
            ins = v.tensor_copy(out=out[:], in_=pb[:])
        else:
            ins = v.tensor_scalar_mul(out=out[:], in0=pb[:], scalar1=float(scale))
        p.vC.inc(ins)
    lstack.close()
    return out



def emit_ln_dmajor(p, xsrc_fn, ntiles, width, dst=None, keep_stats=True,
                   name="ln", rbase=0, double=False, affine=None, stats=None,
                   dst2=None):
    return _emit_ln_dmajor(p, xsrc_fn, ntiles, width, dst, keep_stats,
                           name, rbase, double, affine, stats, dst2)


def _emit_ln_dmajor(p, xsrc_fn, ntiles, width, dst, keep_stats,
                    name, rbase, double, affine, stats, dst2):
    if dst is None:
        dst = p.sbuf(name + "_xT", [128, DG * width], BF16)
    if stats is None:
        m_cols = p.sbuf(name + "_m", [128, ntiles], F32)
        r_cols = p.sbuf(name + "_r", [128, ntiles], F32)
    else:
        m_cols, r_cols = stats

    NXB = 3
    lstack = ExitStack()

    def _ls(nm, shape, dtype):
        return lstack.enter_context(
            p.nc.sbuf_tensor(f"{nm}_{p.uid()}", shape, dtype))
    mm = _ls(name + "_mm", [128, 1], F32)
    xb = [_ls(f"{name}_xb{j}", [128, D], F32) for j in range(NXB)]
    ldx = [p.r[rbase + j] for j in range(NXB)]
    xnb = [_ls(f"{name}_xnb{j}", [128, D], BF16) for j in range(2)]
    sq_junk = _ls(name + "_sqj", [128, D], BF16)
    xf = _ls(name + "_xf", [128, D], F32) if (double or affine) else None
    st2 = _ls(name + "_st2", [128, 4], F32) if double else None
    xnb2 = [_ls(f"{name}_xnb2{j}", [128, D], BF16)
            for j in range(2)] if dst2 is not None else None
    ptr2 = [lstack.enter_context(p.nc.psum_tensor(
        f"{name}_ptr2{j}_{p.uid()}", [128, D], BF16))
        for j in range(2)] if dst2 is not None else None
    tmark2 = [0, 0]
    emark2 = [0, 0]
    ptr = [lstack.enter_context(p.nc.psum_tensor(
        f"{name}_ptr{j}_{p.uid()}", [128, D], BF16)) for j in range(2)]
    vmark = [0] * NXB
    smark = [0] * NXB
    tmark = [0] * 2
    emark = [0] * 2
    tilemarks = [0] * ntiles

    for i in range(ntiles):
        slot = i % NXB
        ps = i % 2
        mc = m_cols[:, i:i + 1]
        rc = r_cols[:, i:i + 1]

        @p.block.sync
        def _(e, i=i, slot=slot):
            if i >= NXB:
                e.wait_ge(p.vC.h, vmark[slot])
                e.wait_ge(p.sC.h, smark[slot])
            ldx[slot].inc(e.dma_start(out=xb[slot][:], in_=xsrc_fn(i)), 16)

        @p.block.scalar
        def _(e, slot=slot, rc=rc):
            e.wait_ge(ldx[slot].h, ldx[slot].n)
            smark[slot] = p.sC.inc(
                e.activation(out=sq_junk[:], in_=xb[slot][:], func=AF.Square,
                             accum_out=rc))

        @p.block.vector
        def _(e, slot=slot, mc=mc, rc=rc):
            def step(ins):
                p.vC.inc(ins)
                e.wait_ge(p.vC.h, p.vC.n)
            e.wait_ge(ldx[slot].h, ldx[slot].n)
            step(e.tensor_reduce(out=mc, in_=xb[slot][:], axis=AX.X, op=ALU.add))
            step(e.tensor_scalar_mul(out=mc, in0=mc, scalar1=1.0 / D))
            step(e.tensor_mul(out=mm[:], in0=mc, in1=mc))
            e.wait_ge(p.sC.h, smark[slot])
            step(e.tensor_scalar(out=rc, in0=rc, scalar1=1.0 / D, scalar2=EPS,
                                 op0=ALU.mult, op1=ALU.add))
            step(e.tensor_sub(out=rc, in0=rc, in1=mm[:]))
            p.vC.inc(e.reciprocal(out=rc, in_=rc))

        @p.block.scalar
        def _(e, rc=rc):
            e.wait_ge(p.vC.h, p.vC.n)
            p.sC.inc(e.sqrt(out=rc, in_=rc))

        @p.block.vector
        def _(e, i=i, slot=slot, ps=ps, mc=mc, rc=rc):
            def step(ins):
                p.vC.inc(ins)
                e.wait_ge(p.vC.h, p.vC.n)
            e.wait_ge(p.sC.h, p.sC.n)
            if i >= 2:
                e.wait_ge(p.tC.h, tmark[ps])
            if not (double or affine):
                vmark[slot] = p.vC.inc(
                    e.tensor_scalar(out=xnb[ps][:], in0=xb[slot][:],
                                    scalar1=mc, scalar2=rc,
                                    op0=ALU.subtract, op1=ALU.mult))
                return
            step(e.tensor_scalar(out=xf[:], in0=xb[slot][:],
                                 scalar1=mc, scalar2=rc,
                                 op0=ALU.subtract, op1=ALU.mult))
            vmark[slot] = p.vC.n
            if dst2 is not None:
                if i >= 2:
                    e.wait_ge(p.tC.h, tmark2[ps])
                step(e.tensor_copy(out=xnb2[ps][:], in_=xf[:]))
            if double:
                step(e.tensor_reduce(out=st2[:, 0:1], in_=xf[:],
                                     axis=AX.X, op=ALU.add))
                step(e.tensor_scalar_mul(out=st2[:, 0:1], in0=st2[:, 0:1],
                                         scalar1=1.0 / D))
                step(e.tensor_mul(out=st2[:, 1:2], in0=st2[:, 0:1],
                                  in1=st2[:, 0:1]))
            if double:
                pass
            if affine is None:
                if double:
                    pass

        if double:
            @p.block.scalar
            def _(e, ps=ps):
                e.wait_ge(p.vC.h, p.vC.n)
                p.sC.inc(e.activation(out=sq_junk[:], in_=xf[:],
                                      func=AF.Square, accum_out=st2[:, 2:3]))

            @p.block.vector
            def _(e, ps=ps):
                def step(ins):
                    p.vC.inc(ins)
                    e.wait_ge(p.vC.h, p.vC.n)
                e.wait_ge(p.sC.h, p.sC.n)
                step(e.tensor_scalar(out=st2[:, 2:3], in0=st2[:, 2:3],
                                     scalar1=1.0 / D, scalar2=EPS,
                                     op0=ALU.mult, op1=ALU.add))
                step(e.tensor_sub(out=st2[:, 2:3], in0=st2[:, 2:3],
                                  in1=st2[:, 1:2]))
                step(e.reciprocal(out=st2[:, 2:3], in_=st2[:, 2:3]))

            @p.block.scalar
            def _(e, ps=ps):
                e.wait_ge(p.vC.h, p.vC.n)
                p.sC.inc(e.sqrt(out=st2[:, 2:3], in_=st2[:, 2:3]))

        if double or affine:
            @p.block.vector
            def _(e, i=i, slot=slot, ps=ps):
                def step(ins):
                    p.vC.inc(ins)
                    e.wait_ge(p.vC.h, p.vC.n)
                e.wait_ge(p.sC.h, p.sC.n)
                if double:
                    step(e.tensor_scalar(out=xf[:], in0=xf[:],
                                         scalar1=st2[:, 0:1],
                                         scalar2=st2[:, 2:3],
                                         op0=ALU.subtract, op1=ALU.mult))
                if affine is not None:
                    g_b, b_b = affine
                    step(e.tensor_mul(out=xf[:], in0=xf[:], in1=g_b[:]))
                    vmark[slot] = p.vC.inc(
                        e.tensor_add(out=xnb[ps][:], in0=xf[:], in1=b_b[:]))
                    e.wait_ge(p.vC.h, p.vC.n)
                else:
                    vmark[slot] = p.vC.inc(
                        e.tensor_copy(out=xnb[ps][:], in_=xf[:]))
                    e.wait_ge(p.vC.h, p.vC.n)

        @p.block.tensor
        def _(e, i=i, ps=ps):
            e.wait_ge(p.vC.h, p.vC.n)
            if i >= 2:
                e.wait_ge(p.sC.h, emark[ps])
            ins = None
            for dg in range(DG):
                ins = e.transpose(ptr[ps][:, dg * 128:(dg + 1) * 128],
                                  xnb[ps][:, dg * 128:(dg + 1) * 128],
                                  p.consts["id_bf"][:])
            tmark[ps] = p.tC.inc(ins)
            if dst2 is not None:
                if i >= 2:
                    e.wait_ge(p.sC.h, emark2[ps])
                ins = None
                for dg in range(DG):
                    ins = e.transpose(ptr2[ps][:, dg * 128:(dg + 1) * 128],
                                      xnb2[ps][:, dg * 128:(dg + 1) * 128],
                                      p.consts["id_bf"][:])
                tmark2[ps] = p.tC.inc(ins)

        @p.block.scalar
        def _(e, i=i, ps=ps):
            e.wait_ge(p.tC.h, tmark[ps])
            out_ap = dst[:].rearrange("p (g w) -> p g w", g=DG)[
                :, :, i * 128:(i + 1) * 128]
            in_ap = ptr[ps][:].rearrange("p (g x) -> p g x", g=DG)
            emark[ps] = p.sC.inc(e.copy(out=out_ap, in_=in_ap))
            tilemarks[i] = emark[ps]
            if dst2 is not None:
                e.wait_ge(p.tC.h, tmark2[ps])
                out_ap2 = dst2[:].rearrange("p (g w) -> p g w", g=DG)[
                    :, :, i * 128:(i + 1) * 128]
                in_ap2 = ptr2[ps][:].rearrange("p (g x) -> p g x", g=DG)
                emark2[ps] = p.sC.inc(e.copy(out=out_ap2, in_=in_ap2))
                tilemarks[i] = emark2[ps]

    for eng in ("tensor", "vector", "scalar", "gpsimd", "sync"):
        @getattr(p.block, eng)
        def _(e):
            e.wait_ge(p.sC.h, p.sC.n)
    lstack.close()
    return dst, m_cols, r_cols, tilemarks



def phase_l1_ffn(p, xsrc, xdst, W):
    phase_begin(p)
    F = 4096
    FT = F // 128
    CH = 512
    NCH = T // CH
    b1cols = load_bias_cols(p, "b1", W["ffn_b1"], F)
    b2b = load_bcast(p, "b2b", W["ffn_b2"], D)
    vready = p.vC.n
    xnT, _, _, lnmarks = emit_ln_dmajor(
        p, lambda i: xsrc[i * 128:(i + 1) * 128, :], TT, T, name="l1x")

    def xnT_view(k, c):
        return xnT[:, k * T + c * CH: k * T + c * CH + CH]

    h1T = p.sbuf("h1T", [128, FT * CH], BF16)
    w1blk = [p.sbuf(f"w1blk{j}", [128, DG * 512], BF16) for j in range(2)]
    w1sem = [p.r[3 + j] for j in range(2)]
    w1mark = [0, 0]
    w2t = [p.sbuf(f"w2t{j}", [128, D], BF16) for j in range(4)]
    w2sem = [p.r[5 + j] for j in range(4)]
    w2mark = [0] * 4
    ph1 = [p.psum(f"ph1{j}", [128, CH], F32) for j in range(2)]
    ph1mark = [0, 0]
    ph2 = [p.psum(f"ph2{j}", [128, D], F32) for j in range(2)]
    ph2mark = [0, 0]
    xrb = [p.sbuf(f"xrb{j}", [128, D], F32) for j in range(2)]
    xrsem = [p.r[9 + j] for j in range(2)]
    xrmark = [0, 0]
    xov = [p.sbuf(f"xov{j}", [128, D], F32) for j in range(2)]
    tmpf = p.sbuf("tmpf", [128, D], F32)
    stsem = [p.r[11 + j] for j in range(2)]

    for c in range(NCH):
        for fb in range(8):
            wslot = fb % 2

            @p.block.gpsimd
            def _(e, c=c, fb=fb, wslot=wslot):
                if c * 8 + fb >= 2:
                    e.wait_ge(p.tC.h, w1mark[wslot])
                for dg in range(DG):
                    w1sem[wslot].inc(e.dma_start(
                        out=w1blk[wslot][:, dg * 512:(dg + 1) * 512],
                        in_=W["ffn_w1"][dg * 128:(dg + 1) * 128,
                                        fb * 512:(fb + 1) * 512]), 16)

            mmdone = [0] * 4
            for ftl in range(4):
                ft = fb * 4 + ftl
                par = ft % 2

                @p.block.tensor
                def _(e, c=c, fb=fb, wslot=wslot, ftl=ftl, ft=ft, par=par):
                    if c == 0 and fb == 0 and ftl == 0:
                        e.wait_ge(p.vC.h, vready)
                    if fb == 0 and ftl == 0:
                        e.wait_ge(p.sC.h, lnmarks[4 * c + 3])
                    if ftl == 0:
                        e.wait_ge(w1sem[wslot].h, w1sem[wslot].n)
                    e.wait_ge(p.sC.h, ph1mark[par])
                    ins = None
                    for k in range(DG):
                        ins = e.matmul(
                            ph1[par][:],
                            w1blk[wslot][:, k * 512 + ftl * 128:
                                         k * 512 + ftl * 128 + 128],
                            xnT_view(k, c), start=(k == 0), stop=(k == DG - 1))
                    mmdone[ftl] = p.tC.inc(ins)
                    if ftl == 3:
                        w1mark[wslot] = mmdone[ftl]

                @p.block.scalar
                def _(e2, c=c, fb=fb, ftl=ftl, ft=ft, par=par):
                    if c == 0 and fb == 0 and ftl == 0:
                        e2.wait_ge(p.vC.h, vready)
                    e2.wait_ge(p.tC.h, mmdone[ftl])
                    ph1mark[par] = p.sC.inc(
                        e2.activation(out=h1T[:, ft * CH:(ft + 1) * CH],
                                      in_=ph1[par][:], func=AF.Gelu,
                                      bias=b1cols[:, ft:ft + 1]))

        sC_h1_done = p.sC.n
        for mtp in range(2):
            for k in range(FT):
                ws = k % 4

                @p.block.gpsimd
                def _(e, c=c, mtp=mtp, k=k, ws=ws):
                    if c > 0 or mtp > 0 or k >= 4:
                        e.wait_ge(p.tC.h, w2mark[ws])
                    w2sem[ws].inc(e.dma_start(
                        out=w2t[ws][:],
                        in_=W["ffn_w2"][k * 128:(k + 1) * 128, :]), 16)

                @p.block.tensor
                def _(e, c=c, mtp=mtp, k=k, ws=ws, sC_h1_done=sC_h1_done):
                    if k == 0:
                        e.wait_ge(p.sC.h, sC_h1_done)
                    e.wait_ge(w2sem[ws].h, w2sem[ws].n)
                    ins = None
                    for mi in range(2):
                        mt = mtp * 2 + mi
                        if k == 0:
                            e.wait_ge(p.vC.h, ph2mark[mi])
                        for n0 in range(0, D, 512):
                            ins = e.matmul(
                                ph2[mi][:, n0:n0 + 512],
                                h1T[:, k * CH + mt * 128:k * CH + mt * 128 + 128],
                                w2t[ws][:, n0:n0 + 512],
                                start=(k == 0), stop=(k == FT - 1))
                    w2mark[ws] = p.tC.inc(ins)

            for mi in range(2):
                mt = mtp * 2 + mi
                tile = c * 4 + mt
                xs = tile % 2

                @p.block.sync
                def _(e, tile=tile, xs=xs):
                    if tile >= 2:
                        e.wait_ge(p.vC.h, xrmark[xs])
                    xrsem[xs].inc(e.dma_start(
                        out=xrb[xs][:],
                        in_=xsrc[tile * 128:(tile + 1) * 128, :]), 16)

                @p.block.vector
                def _(e, mi=mi, tile=tile, xs=xs):
                    e.wait_ge(p.tC.h, p.tC.n)
                    e.wait_ge(xrsem[xs].h, xrsem[xs].n)
                    if tile >= 2:
                        e.wait_ge(stsem[xs].h, stsem[xs].n)
                    ph2mark[mi] = p.vC.inc(
                        e.tensor_add(out=tmpf[:], in0=ph2[mi][:], in1=b2b[:]))
                    e.wait_ge(p.vC.h, p.vC.n)
                    xrmark[xs] = p.vC.inc(
                        e.scalar_tensor_tensor(out=xov[xs][:], in0=tmpf[:],
                                               scalar=0.5, in1=xrb[xs][:],
                                               op0=ALU.mult, op1=ALU.add))

                @p.block.sync
                def _(e, tile=tile, xs=xs):
                    e.wait_ge(p.vC.h, p.vC.n)
                    stsem[xs].inc(e.dma_start(
                        out=xdst[tile * 128:(tile + 1) * 128, :],
                        in_=xov[xs][:]), 16)

    @p.block.sync
    def _(e):
        e.wait_ge(stsem[0].h, stsem[0].n)
        e.wait_ge(stsem[1].h, stsem[1].n)
    phase_end(p)





def phase_l0_attn(p, xsrc, xdst, W):
    phase_begin(p)
    HD, NH, VD = 128, 16, 64
    CHQ = 512

    modscr = p.nc.dram_tensor(f"modscr_{p.uid()}", [16, 64], F32,
                              kind="Internal").ap()
    mprep = ExitStack()

    def _mp(nm, shape, dtype):
        return mprep.enter_context(
            p.nc.sbuf_tensor(f"{nm}_{p.uid()}", shape, dtype))
    ph_sb = _mp("ph", [16, 64], F32)
    am_sb = _mp("am", [16, 64], F32)
    am_sb2 = _mp("am2", [16, 64], F32)
    am_sb3 = _mp("am3", [16, 64], F32)
    halfpi = _mp("halfpi", [16, 1], F32)

    @p.block.sync
    def _(e):
        p.load_serial(e, p.ld, ph_sb[:], W["attn_quantum_phase"])
        p.load_serial(e, p.ld, am_sb[:], W["attn_quantum_amplitude"])

    @p.block.gpsimd
    def _(e):
        p.gC.inc(e.memset(halfpi[:], -math.pi / 2.0))

    @p.block.vector
    def _(e):
        def step(ins):
            p.vC.inc(ins)
            e.wait_ge(p.vC.h, p.vC.n)
        e.wait_ge(p.ld.h, p.ld.n)
        step(e.tensor_single_scalar(out=am_sb2[:], in_=ph_sb[:],
                                    scalar=math.pi, op=ALU.is_gt))
        step(e.tensor_single_scalar(out=am_sb3[:], in_=ph_sb[:],
                                    scalar=-math.pi, op=ALU.is_lt))
        step(e.tensor_sub(out=am_sb2[:], in0=am_sb2[:], in1=am_sb3[:]))
        step(e.scalar_tensor_tensor(out=ph_sb[:], in0=am_sb2[:],
                                    scalar=-2 * math.pi, in1=ph_sb[:],
                                    op0=ALU.mult, op1=ALU.add))
    wmod_mark = p.vC.n

    @p.block.scalar
    def _(e):
        e.wait_ge(p.vC.h, wmod_mark)
        e.wait_ge(p.gC.h, p.gC.n)
        p.sC.inc(e.activation(out=ph_sb[:], in_=ph_sb[:], func=AF.Sin,
                              scale=0.5))
        e.wait_ge(p.sC.h, p.sC.n)
        p.sC.inc(e.square(out=ph_sb[:], in_=ph_sb[:]))

    @p.block.vector
    def _(e):
        def step(ins):
            p.vC.inc(ins)
            e.wait_ge(p.vC.h, p.vC.n)
        e.wait_ge(p.sC.h, p.sC.n)
        step(e.tensor_scalar(out=ph_sb[:], in0=ph_sb[:], scalar1=-2.0,
                             scalar2=1.0, op0=ALU.mult, op1=ALU.add))
        step(e.tensor_mul(out=ph_sb[:], in0=ph_sb[:], in1=am_sb[:]))

    @p.block.sync
    def _(e):
        e.wait_ge(p.vC.h, p.vC.n)
        p.load_serial(e, p.ld, modscr, ph_sb[:])
    mprep.close()
    for eng in ("tensor", "vector", "scalar", "gpsimd"):
        @getattr(p.block, eng)
        def _(e):
            e.wait_ge(p.ld.h, p.ld.n)

    wq = p.sbuf("wqm", [128, DG * D], BF16)
    wk = p.sbuf("wkm", [128, DG * D], BF16)
    wv = p.sbuf("wvm", [128, DG * D], BF16)
    wo = p.sbuf("wo", [128, DG * D], BF16)
    prep2 = ExitStack()
    mod_b = load_bcast(p, "modb", modscr.rearrange("h d -> (h d)"), D,
                       stack=prep2)
    wtmp = [prep2.enter_context(p.nc.sbuf_tensor(
        f"wtmp{j}_{p.uid()}", [128, D], F32)) for j in range(2)]
    wtsem = [p.r[17 + j] for j in range(2)]
    wtm = [0, 0]

    @p.block.gpsimd
    def _(e):
        for k in range(DG):
            p.load_serial(e, p.ldg, wo[:, k * D:(k + 1) * D],
                          W["attn_Wo"][k * 128:(k + 1) * 128, :])

    for wi, (wdst, wsrc) in enumerate(
            ((wq, W["attn_Wq"]), (wk, W["attn_Wk"]), (wv, W["attn_Wv"]))):
        for k in range(DG):
            sl = (wi * DG + k) % 2

            @p.block.sync
            def _(e, wsrc=wsrc, k=k, sl=sl, wi=wi):
                if wi * DG + k >= 2:
                    e.wait_ge(p.vC.h, wtm[sl])
                wtsem[sl].inc(e.dma_start(
                    out=wtmp[sl][:], in_=wsrc[k * 128:(k + 1) * 128, :]), 16)

            @p.block.vector
            def _(e, wdst=wdst, k=k, sl=sl):
                e.wait_ge(wtsem[sl].h, wtsem[sl].n)
                wtm[sl] = p.vC.inc(e.tensor_mul(
                    out=wdst[:, k * D:(k + 1) * D], in0=wtmp[sl][:],
                    in1=mod_b[:]))

    for eng in ("tensor", "vector", "scalar", "gpsimd", "sync"):
        @getattr(p.block, eng)
        def _(e):
            e.wait_ge(p.vC.h, p.vC.n)
    prep2.close()

    cosb = p.sbuf("cosb", [128, S], BF16)
    sinb = p.sbuf("sinb", [128, S], BF16)
    cstk = ExitStack()
    crow = cstk.enter_context(p.nc.sbuf_tensor(
        f"crow_{p.uid()}", [1, 2 * S], F32))
    cbias = cstk.enter_context(p.nc.sbuf_tensor(
        f"cbias_{p.uid()}", [1, 2], F32))

    @p.block.gpsimd
    def _(e):
        e.wait_ge(p.gC.h, p.gC.n)
        p.gC.inc(e.iota(out=crow[:, 0:S], pattern=[[1, S]], base=0,
                        channel_multiplier=0,
                        allow_small_or_imprecise_dtypes=True))
        e.wait_ge(p.gC.h, p.gC.n)
        e.memset(cbias[:, 0:1], -math.pi / 2.0)
        p.gC.inc(e.memset(cbias[:, 1:2], -math.pi))

    @p.block.scalar
    def _(e):
        e.wait_ge(p.gC.h, p.gC.n)
        p.sC.inc(e.activation(out=crow[:, S:2 * S], in_=crow[:, 0:S],
                              func=AF.Sin, bias=cbias[:, 0:1],
                              scale=math.pi / S))
        e.wait_ge(p.sC.h, p.sC.n)
        p.sC.inc(e.square(out=crow[:, S:2 * S], in_=crow[:, S:2 * S]))
        e.wait_ge(p.sC.h, p.sC.n)
        p.sC.inc(e.activation(out=crow[:, 0:S], in_=crow[:, 0:S],
                              func=AF.Sin, bias=cbias[:, 1:2],
                              scale=2 * math.pi / S))

    psc = cstk.enter_context(p.nc.psum_tensor(f"psc_{p.uid()}", [128, S], F32))

    @p.block.tensor
    def _(e):
        e.wait_ge(p.sC.h, p.sC.n)
        ins = None
        for j0 in range(0, S, 512):
            ins = e.matmul(psc[:, j0:j0 + 512], p.consts["ones_f"][0:1, 0:128],
                           crow[:, S + j0:S + j0 + 512], start=True, stop=True)
        p.tC.inc(ins)

    @p.block.vector
    def _(e):
        e.wait_ge(p.tC.h, p.tC.n)
        p.vC.inc(e.tensor_scalar(out=cosb[:], in0=psc[:], scalar1=2.0,
                                 scalar2=-1.0, op0=ALU.mult, op1=ALU.add))
        e.wait_ge(p.vC.h, p.vC.n)

    @p.block.tensor
    def _(e):
        e.wait_ge(p.vC.h, p.vC.n)
        ins = None
        for j0 in range(0, S, 512):
            ins = e.matmul(psc[:, j0:j0 + 512], p.consts["ones_f"][0:1, 0:128],
                           crow[:, j0:j0 + 512], start=True, stop=True)
        p.tC.inc(ins)

    @p.block.vector
    def _(e):
        e.wait_ge(p.tC.h, p.tC.n)
        p.vC.inc(e.tensor_scalar_mul(out=sinb[:], in0=psc[:], scalar1=-1.0))
        e.wait_ge(p.vC.h, p.vC.n)

    for eng in ("tensor", "scalar", "gpsimd", "sync"):
        @getattr(p.block, eng)
        def _(e):
            e.wait_ge(p.vC.h, p.vC.n)
    cstk.close()

    smat = p.sbuf("smat", [128, 256], BF16)

    @p.block.gpsimd
    def _(e):
        p.gC.inc(e.memset(smat[:], SQ2I))
        e.wait_ge(p.gC.h, p.gC.n)
        e.affine_select(out=smat[:, 0:128], in_=smat[:, 0:128],
                        pattern=[[1, 128]], compare_op=ALU.is_equal,
                        fill=0.0, base=0, channel_multiplier=-1)
        p.gC.inc(e.engine_nop())
        e.wait_ge(p.gC.h, p.gC.n)
        e.affine_select(out=smat[:, 128:256], in_=smat[:, 128:256],
                        pattern=[[1, 128]], compare_op=ALU.is_equal,
                        fill=0.0, base=1, channel_multiplier=-1)
        p.gC.inc(e.engine_nop())

    @p.block.vector
    def _(e):
        e.wait_ge(p.gC.h, p.gC.n)
        p.vC.inc(e.tensor_add(out=smat[:, 0:128], in0=smat[:, 0:128],
                              in1=smat[:, 128:256]))
        e.wait_ge(p.vC.h, p.vC.n)

    @p.block.gpsimd
    def _(e):
        e.wait_ge(p.vC.h, p.vC.n)
        p.gC.inc(e.memset(smat[:, 128:256], SQ2I))
        e.wait_ge(p.gC.h, p.gC.n)
        e.affine_select(out=smat[:, 128:256], in_=smat[:, 128:256],
                        pattern=[[1, 128]], compare_op=ALU.is_equal,
                        fill=0.0, base=-127, channel_multiplier=-128)
        p.gC.inc(e.engine_nop())
    smat_mark = p.gC.n

    xnT = p.sbuf("xnTb", [128, DG * S], BF16)
    vaug = p.sbuf("vaug", [128, 16 * NH * (VD + 1)], BF16)
    attnT = p.sbuf("attnT", [128, DG * S], BF16)
    vtmp = p.sbuf("vtmp", [128, 3 * D], BF16)
    v0row = p.sbuf("v0row", [1, D], BF16)

    @p.block.gpsimd
    def _(e):
        e.wait_ge(p.gC.h, p.gC.n)
        va = vaug[:].rearrange("q (t h c) -> q t h c", t=16, h=NH)
        p.gC.inc(e.memset(va[:, :, :, VD:VD + 1], 1.0))
    ones_mark = p.gC.n

    for b in range(B_PER_CORE):
        xnT_, mcols, rcols, lnmarks = emit_ln_dmajor(
            p, lambda i, b=b: xsrc[(16 * b + i) * 128:(16 * b + i + 1) * 128, :],
            16, S, dst=xnT, name=f"lnb{b}")

        def xnT_v(k, j0, w):
            return xnT[:, k * S + j0:k * S + j0 + w]

        vstk = ExitStack()
        psv = [vstk.enter_context(p.nc.psum_tensor(
            f"psv{b}{j}_{p.uid()}", [128, D], F32)) for j in range(2)]
        psv2 = [vstk.enter_context(p.nc.psum_tensor(
            f"psv2{b}{j}_{p.uid()}", [128, D], F32)) for j in range(2)]
        psvmark = [0, 0]
        psv2mark = [0, 0]
        evmark = [0] * 16
        for step in range(18):
            if step < 16:
                mt = step
                pv = mt % 2

                @p.block.tensor
                def _(e, b=b, mt=mt, pv=pv):
                    if mt == 0:
                        e.wait_ge(p.gC.h, ones_mark)
                        e.wait_ge(p.ldg.h, p.ldg.n)
                        e.wait_ge(p.vC.h, p.vC.n)
                    e.wait_ge(p.sC.h, lnmarks[mt])
                    e.wait_ge(p.sC.h, psvmark[pv])
                    ins = None
                    for k in range(DG):
                        for n0 in (0, 512):
                            ins = e.matmul(
                                psv[pv][:, n0:n0 + 512],
                                xnT[:, k * S + mt * 128:k * S + mt * 128 + 128],
                                wv[:, k * D + n0:k * D + n0 + 512],
                                start=(k == 0), stop=(k == DG - 1))
                    p.tC.inc(ins)
                vm = p.tC.n

                @p.block.scalar
                def _(e, mt=mt, pv=pv, vm=vm):
                    e.wait_ge(p.tC.h, vm)
                    evmark[mt] = p.sC.inc(e.copy(
                        out=vtmp[:, (mt % 3) * D:(mt % 3 + 1) * D],
                        in_=psv[pv][:]))
                    psvmark[pv] = evmark[mt]
                    if mt == 0:
                        e.wait_ge(p.sC.h, p.sC.n)
                        p.sC.inc(e.copy(out=v0row[:], in_=vtmp[0:1, 0:D]))

            if step >= 2:
                kt = step - 2
                p2 = kt % 2

                @p.block.tensor
                def _(e, b=b, kt=kt, p2=p2):
                    e.wait_ge(p.sC.h, evmark[min(kt + 1, 15)])
                    e.wait_ge(p.vC.h, psv2mark[p2])
                    ins = None
                    for n0 in (0, 512):
                        ins = e.matmul(psv2[p2][:, n0:n0 + 512],
                                       smat[:, 0:128],
                                       vtmp[:, (kt % 3) * D + n0:
                                            (kt % 3) * D + n0 + 512],
                                       start=True, stop=False)
                        if kt < 15:
                            ins = e.matmul(
                                psv2[p2][:, n0:n0 + 512], smat[:, 128:256],
                                vtmp[:, ((kt + 1) % 3) * D + n0:
                                     ((kt + 1) % 3) * D + n0 + 512],
                                start=False, stop=True)
                        else:
                            ins = e.matmul(
                                psv2[p2][:, n0:n0 + 512],
                                smat[0:1, 128:256],
                                v0row[:, n0:n0 + 512],
                                start=False, stop=True)
                    p.tC.inc(ins)
                vm2 = p.tC.n

                @p.block.vector
                def _(e, kt=kt, p2=p2, vm2=vm2):
                    e.wait_ge(p.tC.h, vm2)
                    ov = vaug[:, kt * NH * (VD + 1):(kt + 1) * NH * (VD + 1)]
                    ov = ov.rearrange("q (h c) -> q h c", h=NH)[:, :, 0:VD]
                    iv = psv2[p2][:].rearrange("q (h d) -> q h d", h=NH)
                    psv2mark[p2] = p.vC.inc(e.tensor_copy(out=ov, in_=iv))
                    e.wait_ge(p.vC.h, p.vC.n)

        @p.block.tensor
        def _(e):
            e.wait_ge(p.vC.h, p.vC.n)
        vstk.close()

        hstk = ExitStack()

        def _hs(nm, shape, dtype):
            return hstk.enter_context(
                p.nc.sbuf_tensor(f"{nm}_{p.uid()}", shape, dtype))
        qT = _hs("qTh", [HD, S], BF16)
        kT = _hs("kTh", [HD, S], BF16)
        qsq = _hs("qsq", [HD, S], BF16)
        cc = _hs("cc", [1, 8], F32)
        negc = _hs("negc", [128, 1], F32)
        pexp = [_hs(f"pexp{j}", [128, 2 * CHQ], BF16) for j in range(2)]
        tmps = [_hs(f"tmps{j}", [128, 2 * CHQ], F32) for j in range(2)]
        recb = _hs("recb", [VD, CHQ], F32)
        psA = [hstk.enter_context(p.nc.psum_tensor(
            f"psA{b}{j}_{p.uid()}", [128, 2 * CHQ], F32)) for j in range(2)]
        psO = [hstk.enter_context(p.nc.psum_tensor(
            f"psO{b}{j}_{p.uid()}", [VD + 1, CHQ], F32)) for j in range(2)]
        psR = hstk.enter_context(p.nc.psum_tensor(
            f"psR{b}_{p.uid()}", [128, CHQ], F32))
        psq = hstk.enter_context(p.nc.psum_tensor(
            f"psq{b}_{p.uid()}", [VD, CHQ], F32))
        pqmark = [0]

        def proj_qk(h, qT_, kT_, b=b):
            for which in (0, 1):
                dstT = qT_ if which == 0 else kT_
                wsel = wq if which == 0 else wk
                for ci4 in range(S // 512):
                    @p.block.tensor
                    def _(e, h=h, wsel=wsel, ci4=ci4):
                        e.wait_ge(p.vC.h, pqmark[0])
                        ins = None
                        for k in range(DG):
                            ins = e.matmul(
                                psq[:, 0:512],
                                wsel[:, k * D + h * VD:k * D + h * VD + VD],
                                xnT_v(k, ci4 * 512, 512),
                                start=(k == 0), stop=(k == DG - 1))
                        p.tC.inc(ins)
                    pm = p.tC.n

                    @p.block.vector
                    def _(e, dstT=dstT, ci4=ci4, pm=pm):
                        e.wait_ge(p.tC.h, pm)
                        c0 = ci4 * 512
                        p.vC.inc(e.tensor_mul(
                            out=dstT[0:64, c0:c0 + 512], in0=psq[:, 0:512],
                            in1=cosb[0:64, c0:c0 + 512]))
                        e.wait_ge(p.vC.h, p.vC.n)
                        pqmark[0] = p.vC.inc(e.tensor_mul(
                            out=dstT[64:128, c0:c0 + 512], in0=psq[:, 0:512],
                            in1=sinb[0:64, c0:c0 + 512]))
                        e.wait_ge(p.vC.h, p.vC.n)

        def vslice(kt, h):
            base = kt * NH * (VD + 1) + h * (VD + 1)
            return vaug[:, base:base + VD + 1]

        def attn_slice(h, c0):
            return attnT[(h % 2) * 64:(h % 2) * 64 + 64,
                         (h // 2) * S + c0:(h // 2) * S + c0 + CHQ]

        cfg = dict(HD=HD, VD=VD, NH=NH, S=S, CHQ=CHQ, qT=qT, kT=kT, qsq=qsq,
                   cc=cc, negc=negc, psA=psA, psO=psO, psR=psR, pexp=pexp,
                   tmps=tmps, recb=recb, proj_qk=proj_qk, vslice=vslice,
                   bias_slice=lambda h, kt, c0: None, bias_ready=None,
                   bias_wait=None, bias_slice_any=False,
                   mask=lambda kt, c0: None, extra_c=None,
                   scale=1.0 / 8.0, attn_slice=attn_slice)
        emit_mha_batch(p, b, cfg)

        for eng in ("tensor", "vector", "scalar", "gpsimd", "sync"):
            @getattr(p.block, eng)
            def _(e):
                e.wait_ge(p.vC.h, p.vC.n)
        hstk.close()

        wstk = ExitStack()

        def _ws(nm, shape, dtype):
            return wstk.enter_context(
                p.nc.sbuf_tensor(f"{nm}_{p.uid()}", shape, dtype))
        xrb = [_ws(f"xrb{j}", [128, D], F32) for j in range(2)]
        xov = [_ws(f"xov{j}", [128, D], F32) for j in range(2)]
        xrsem = [p.r[9 + j] for j in range(2)]
        xrmark = [0, 0]
        stsem = [p.r[11 + j] for j in range(2)]
        psres = [wstk.enter_context(p.nc.psum_tensor(
            f"psres{b}{j}_{p.uid()}", [128, D], F32)) for j in range(2)]
        resmark = [0, 0]
        for mt in range(16):
            tile = 16 * b + mt
            pr = mt % 2
            xs = mt % 2

            @p.block.sync
            def _(e, tile=tile, xs=xs):
                e.wait_ge(p.vC.h, xrmark[xs])
                e.wait_ge(stsem[xs].h, stsem[xs].n)
                xrsem[xs].inc(e.dma_start(
                    out=xrb[xs][:],
                    in_=xsrc[tile * 128:(tile + 1) * 128, :]), 16)

            @p.block.tensor
            def _(e, b=b, mt=mt, pr=pr):
                e.wait_ge(p.vC.h, resmark[pr])
                ins = None
                for dt in range(DG):
                    for n0 in (0, 512):
                        ins = e.matmul(
                            psres[pr][:, n0:n0 + 512],
                            attnT[:, dt * S + mt * 128:dt * S + mt * 128 + 128],
                            wo[:, dt * D + n0:dt * D + n0 + 512],
                            start=(dt == 0), stop=(dt == DG - 1))
                p.tC.inc(ins)
            wm = p.tC.n

            @p.block.vector
            def _(e, tile=tile, xs=xs, pr=pr, wm=wm):
                e.wait_ge(p.tC.h, wm)
                e.wait_ge(xrsem[xs].h, xrsem[xs].n)
                resmark[pr] = p.vC.inc(e.tensor_add(
                    out=xov[xs][:], in0=psres[pr][:], in1=xrb[xs][:]))
                e.wait_ge(p.vC.h, p.vC.n)
                xrmark[xs] = p.vC.n

            @p.block.sync
            def _(e, tile=tile, xs=xs):
                e.wait_ge(p.vC.h, p.vC.n)
                stsem[xs].inc(e.dma_start(
                    out=xdst[tile * 128:(tile + 1) * 128, :],
                    in_=xov[xs][:]), 16)

        @p.block.sync
        def _(e):
            e.wait_ge(stsem[0].h, stsem[0].n)
            e.wait_ge(stsem[1].h, stsem[1].n)

        for eng in ("tensor", "vector", "scalar", "gpsimd"):
            @getattr(p.block, eng)
            def _(e):
                e.wait_ge(stsem[0].h, stsem[0].n)
                e.wait_ge(stsem[1].h, stsem[1].n)
        wstk.close()
    phase_end(p)




def emit_weights_to_bf16(p, src_ap, dst_ap, nrow, ncol, name="wprep"):
    lstk = ExitStack()
    buf = [lstk.enter_context(p.nc.sbuf_tensor(
        f"{name}{j}_{p.uid()}", [128, ncol], BF16)) for j in range(2)]
    bsem = [p.r[19 + j] for j in range(2)]
    osem = [p.r[21 + j] for j in range(2)]
    for i in range(nrow // 128):
        sl = i % 2

        @p.block.gpsimd
        def _(e, i=i, sl=sl):
            e.wait_ge(osem[sl].h, osem[sl].n)
            bsem[sl].inc(e.dma_start(
                out=buf[sl][:], in_=src_ap[i * 128:(i + 1) * 128, :]), 16)

        @p.block.sync
        def _(e, i=i, sl=sl):
            e.wait_ge(bsem[sl].h, bsem[sl].n)
            osem[sl].inc(e.dma_start(
                out=dst_ap[i * 128:(i + 1) * 128, :], in_=buf[sl][:]), 16)

    @p.block.gpsimd
    def _(e):
        e.wait_ge(osem[0].h, osem[0].n)
        e.wait_ge(osem[1].h, osem[1].n)

    @p.block.sync
    def _(e):
        e.wait_ge(osem[0].h, osem[0].n)
        e.wait_ge(osem[1].h, osem[1].n)
    lstk.close()



def emit_router_gates(p, xdm, rw_ap, rb_ap, gates, gatesT, use_eps,
                      name="rt", ntiles=TT, limit=None):
    lstk = ExitStack()
    rw_sb = lstk.enter_context(p.nc.sbuf_tensor(
        f"{name}_rw_{p.uid()}", [128, DG * 8], BF16))
    rb_b = None

    @p.block.gpsimd
    def _(e):
        p.load_serial(e, p.ldg, rw_sb[:].rearrange("q (k e) -> q k e", k=DG),
                      rw_ap.rearrange("(k q) e -> q k e", q=128))
    if rb_ap is not None:
        rb_b = load_bcast(p, name + "_rb", rb_ap, 8, stack=lstk)
    if limit == "loads":
        for eng in ("tensor", "vector", "scalar", "gpsimd", "sync"):
            @getattr(p.block, eng)
            def _(e):
                e.wait_ge(p.ldg.h, p.ldg.n)
                e.wait_ge(p.vC.h, p.vC.n)
        lstk.close()
        return
    psg = [lstk.enter_context(p.nc.psum_tensor(
        f"{name}_psg{j}_{p.uid()}", [128, 8], F32)) for j in range(2)]
    psgt = lstk.enter_context(p.nc.psum_tensor(
        f"{name}_psgt_{p.uid()}", [8, ntiles * 128], F32))
    scr = lstk.enter_context(p.nc.sbuf_tensor(
        f"{name}_scr_{p.uid()}", [128, 12], F32))
    evmark = [0, 0]
    ldm = p.ld.n
    Wd = ntiles * 128

    for mt in range(ntiles):
        pg = mt % 2
        gg = gates[:, mt * 8:(mt + 1) * 8]

        @p.block.tensor
        def _(e, mt=mt, pg=pg):
            if mt == 0:
                e.wait_ge(p.ldg.h, p.ldg.n)
                e.wait_ge(p.vC.h, p.vC.n)
            if mt >= 2:
                e.wait_ge(p.vC.h, evmark[pg])
            ins = None
            for k in range(DG):
                ins = e.matmul(psg[pg][:],
                               xdm[:, k * Wd + mt * 128:k * Wd + mt * 128 + 128],
                               rw_sb[:, k * 8:(k + 1) * 8],
                               start=(k == 0), stop=(k == DG - 1))
            p.tC.inc(ins)
        gm = p.tC.n

        @p.block.vector
        def _(e, mt=mt, pg=pg, gg=gg, gm=gm):
            def step(ins):
                p.vC.inc(ins)
                e.wait_ge(p.vC.h, p.vC.n)
            e.wait_ge(p.tC.h, gm)
            if rb_b is not None:
                step(e.tensor_add(out=gg, in0=psg[pg][:], in1=rb_b[:, 0:8]))
            else:
                step(e.tensor_copy(out=gg, in_=psg[pg][:]))
            step(e.reduce_max(out=scr[:, 0:1], in_=gg, axis=AX.X, negate=True))
        vm = p.vC.n

        lvl = {"mm": 1, "exp": 2}.get(limit, 9)
        if lvl >= 2:
            @p.block.scalar
            def _(e, gg=gg, vm=vm):
                e.wait_ge(p.vC.h, vm)
                p.sC.inc(e.activation(out=gg, in_=gg, func=AF.Exp,
                                      bias=scr[:, 0:1], accum_out=scr[:, 1:2]))
        sm = p.sC.n

        @p.block.vector
        def _(e, mt=mt, pg=pg, gg=gg, sm=sm, lvl=lvl):
            def step(ins):
                p.vC.inc(ins)
                e.wait_ge(p.vC.h, p.vC.n)
            e.wait_ge(p.sC.h, sm)
            if lvl < 9:
                evmark[pg] = p.vC.inc(e.tensor_scalar_mul(
                    out=gg, in0=gg, scalar1=1.0))
                e.wait_ge(p.vC.h, p.vC.n)
                return
            step(e.reduce_max(out=scr[:, 2:3], in_=gg, axis=AX.X))
            step(e.tensor_single_scalar(out=scr[:, 4:12], in_=gg,
                                        scalar=scr[:, 2:3], op=ALU.is_equal))
            step(e.scalar_tensor_tensor(out=scr[:, 4:12], in0=scr[:, 4:12],
                                        scalar=-1e30, in1=gg,
                                        op0=ALU.mult, op1=ALU.add))
            step(e.reduce_max(out=scr[:, 3:4], in_=scr[:, 4:12], axis=AX.X))
            step(e.tensor_single_scalar(out=scr[:, 4:12], in_=gg,
                                        scalar=scr[:, 3:4], op=ALU.is_ge))
            step(e.tensor_add(out=scr[:, 2:3], in0=scr[:, 2:3],
                              in1=scr[:, 3:4]))
            if use_eps:
                step(e.scalar_tensor_tensor(out=scr[:, 2:3], in0=scr[:, 1:2],
                                            scalar=1e-8, in1=scr[:, 2:3],
                                            op0=ALU.mult, op1=ALU.add))
            step(e.reciprocal(out=scr[:, 2:3], in_=scr[:, 2:3]))
            step(e.tensor_mul(out=gg, in0=gg, in1=scr[:, 4:12]))
            evmark[pg] = p.vC.inc(e.tensor_scalar_mul(
                out=gg, in0=gg, scalar1=scr[:, 2:3]))
            e.wait_ge(p.vC.h, p.vC.n)
    gv = p.vC.n
    if limit in ("gates", "mm", "exp"):
        for eng in ("tensor", "vector", "scalar", "gpsimd", "sync"):
            @getattr(p.block, eng)
            def _(e):
                e.wait_ge(p.vC.h, gv)
        lstk.close()
        return

    @p.block.tensor
    def _(e):
        e.wait_ge(p.vC.h, gv)
        ins = None
        for mt in range(ntiles):
            ins = e.transpose(psgt[:, mt * 128:(mt + 1) * 128],
                              gates[:, mt * 8:(mt + 1) * 8],
                              p.consts["id_f32"][:])
        p.tC.inc(ins)

    @p.block.vector
    def _(e):
        e.wait_ge(p.tC.h, p.tC.n)
        p.vC.inc(e.tensor_copy(out=gatesT[:], in_=psgt[:]))
        e.wait_ge(p.vC.h, p.vC.n)

    for eng in ("tensor", "scalar", "gpsimd", "sync"):
        @getattr(p.block, eng)
        def _(e):
            e.wait_ge(p.vC.h, p.vC.n)
    lstk.close()


def emit_moe(p, xdm, gates, gatesT, W1_src, B1_cols, W2_src, B2_sb, Fdim, act,
             final_fn, pre_expert=None, premov=None, name="moe",
             tile0=0, ntiles=TT):
    E = 8
    CH = 512
    Wd = ntiles * 128
    NCH = Wd // CH
    FT = Fdim // 128
    FB = Fdim // 512

    if premov is None:
        def premov(e_, c, k, j0, w):
            return xdm[:, k * Wd + c * CH + j0:k * Wd + c * CH + j0 + w]

    lstk = ExitStack()
    h1T = lstk.enter_context(p.nc.sbuf_tensor(
        f"{name}_h1T_{p.uid()}", [128, FT * CH], BF16))
    acc = lstk.enter_context(p.nc.sbuf_tensor(
        f"{name}_acc_{p.uid()}", [128, 4 * D], F32))
    w1blk = [lstk.enter_context(p.nc.sbuf_tensor(
        f"{name}_w1b{j}_{p.uid()}", [128, DG * 512], BF16)) for j in range(2)]
    w1sem = [p.r[3 + j] for j in range(2)]
    w1mark = [0, 0]
    w2t = [lstk.enter_context(p.nc.sbuf_tensor(
        f"{name}_w2t{j}_{p.uid()}", [128, 512], BF16)) for j in range(4)]
    w2sem = [p.r[5 + j] for j in range(4)]
    w2mark = [0] * 4
    ph1 = [lstk.enter_context(p.nc.psum_tensor(
        f"{name}_ph1{j}_{p.uid()}", [128, CH], F32)) for j in range(2)]
    ph1mark = [0, 0]
    ph2 = [lstk.enter_context(p.nc.psum_tensor(
        f"{name}_ph2{j}_{p.uid()}", [128, 512], F32)) for j in range(4)]
    ph2mark = [0, 0, 0, 0]
    wseq = [0]

    for c in range(NCH):
        for mi in range(4):
            mtg = c * 4 + mi
            pm = mi % 2

            @p.block.tensor
            def _(e, mtg=mtg, pm=pm, c=c, mi=mi):
                if c == 0 and mi == 0:
                    e.wait_ge(p.vC.h, p.vC.n)
                    e.wait_ge(p.ld.h, p.ld.n)
                e.wait_ge(p.vC.h, ph2mark[pm])
                e.wait_ge(p.vC.h, ph2mark[pm + 2])
                e.matmul(ph2[pm][:],
                         gatesT[:, mtg * 128:(mtg + 1) * 128],
                         B2_sb[:, 0:512], start=True, stop=True)
                ins = e.matmul(ph2[pm + 2][:],
                               gatesT[:, mtg * 128:(mtg + 1) * 128],
                               B2_sb[:, 512:1024], start=True, stop=True)
                p.tC.inc(ins)
            bm = p.tC.n

            @p.block.vector
            def _(e, mi=mi, pm=pm, bm=bm):
                e.wait_ge(p.tC.h, bm)
                e.wait_ge(p.vC.h, p.vC.n)
                ph2mark[pm] = p.vC.inc(e.tensor_copy(
                    out=acc[:, mi * D:mi * D + 512], in_=ph2[pm][:]))
                e.wait_ge(p.vC.h, p.vC.n)
                ph2mark[pm + 2] = p.vC.inc(e.tensor_copy(
                    out=acc[:, mi * D + 512:(mi + 1) * D], in_=ph2[pm + 2][:]))

        pvm = [0]
        for ex in range(E):
            if pre_expert is not None:
                pre_expert(ex, c)
                pvm[0] = p.vC.n
            for fb in range(FB):
                wslot = wseq[0] % 2
                wseq[0] += 1

                @p.block.gpsimd
                def _(e, ex=ex, c=c, fb=fb, wslot=wslot):
                    e.wait_ge(p.tC.h, w1mark[wslot])
                    for dg in range(DG):
                        w1sem[wslot].inc(e.dma_start(
                            out=w1blk[wslot][:, dg * 512:(dg + 1) * 512],
                            in_=W1_src(ex, dg, fb)), 16)

                mmdone = [0] * 4
                for ftl in range(4):
                    ft = fb * 4 + ftl
                    par = ft % 2

                    @p.block.tensor
                    def _(e, ex=ex, c=c, fb=fb, ftl=ftl, ft=ft, par=par,
                          wslot=wslot, pv0=None):
                        if ftl == 0:
                            e.wait_ge(w1sem[wslot].h, w1sem[wslot].n)
                            if pre_expert is not None:
                                e.wait_ge(p.vC.h, pvm[0])
                        e.wait_ge(p.sC.h, ph1mark[par])
                        ins = None
                        for k in range(DG):
                            ins = e.matmul(
                                ph1[par][:],
                                w1blk[wslot][:, k * 512 + ftl * 128:
                                             k * 512 + ftl * 128 + 128],
                                premov(ex, c, k, 0, CH),
                                start=(k == 0), stop=(k == DG - 1))
                        mm_ = p.tC.inc(ins)
                        if ftl == 3:
                            w1mark[wslot] = mm_
                        mmdone[ftl] = mm_

                    @p.block.scalar
                    def _(e, ex=ex, ft=ft, ftl=ftl, par=par):
                        e.wait_ge(p.tC.h, mmdone[ftl])
                        ph1mark[par] = p.sC.inc(e.activation(
                            out=h1T[:, ft * CH:(ft + 1) * CH],
                            in_=ph1[par][:], func=act,
                            bias=B1_cols[:, ex * FT + ft:ex * FT + ft + 1]))
            sC_h1 = p.sC.n

            for dh in range(2):
                n0 = dh * 512
                for k in range(FT):
                    ws = k % 4
                    qeng = "sync" if dh == 0 else "scalar"

                    @getattr(p.block, qeng)
                    def _(e, ex=ex, c=c, k=k, ws=ws, dh=dh, n0=n0):
                        e.wait_ge(p.tC.h, w2mark[ws])
                        w2sem[ws].inc(e.dma_start(
                            out=w2t[ws][:, 0:512],
                            in_=W2_src(ex, k)[:, n0:n0 + 512]), 16)

                    @p.block.tensor
                    def _(e, c=c, dh=dh, k=k, ws=ws, n0=n0, sC_h1=sC_h1):
                        if k == 0 and dh == 0:
                            e.wait_ge(p.sC.h, sC_h1)
                        e.wait_ge(w2sem[ws].h, w2sem[ws].n)
                        ins = None
                        for mi in range(4):
                            if k == 0:
                                e.wait_ge(p.vC.h, ph2mark[mi])
                            ins = e.matmul(
                                ph2[mi][:],
                                h1T[:, k * CH + mi * 128:
                                    k * CH + mi * 128 + 128],
                                w2t[ws][:, 0:512],
                                start=(k == 0), stop=(k == FT - 1))
                        w2mark[ws] = p.tC.inc(ins)
                hm = p.tC.n

                for mi in range(4):
                    @p.block.vector
                    def _(e, ex=ex, c=c, mi=mi, dh=dh, n0=n0, hm=hm):
                        e.wait_ge(p.tC.h, hm)
                        gcol = gates[:, (c * 4 + mi) * 8 + ex:
                                     (c * 4 + mi) * 8 + ex + 1]
                        dsta = acc[:, mi * D + n0:mi * D + n0 + 512]
                        ph2mark[mi] = p.vC.inc(e.scalar_tensor_tensor(
                            out=dsta, in0=ph2[mi][:], scalar=gcol, in1=dsta,
                            op0=ALU.mult, op1=ALU.add))
                        e.wait_ge(p.vC.h, p.vC.n)

        for mi in range(4):
            final_fn(tile0 + c * 4 + mi, acc[:, mi * D:(mi + 1) * D])

    for eng in ("tensor", "vector", "scalar", "gpsimd", "sync"):
        @getattr(p.block, eng)
        def _(e):
            e.wait_ge(p.vC.h, p.vC.n)
    lstk.close()




def load_bias_cols_w(p, name, src_ap, n, w):
    nt = n // w
    if src_ap.ndim == 1:
        src_ap = src_ap.unsqueeze(0)
    cols = p.sbuf(name + "_cols", [w, nt], F32)
    lstack = ExitStack()
    row = lstack.enter_context(p.nc.sbuf_tensor(
        f"{name}_row_{p.uid()}", [1, n], F32))
    pt = lstack.enter_context(p.nc.psum_tensor(
        f"{name}_pt_{p.uid()}", [w, max(nt, 2)], F32))

    @p.block.sync
    def _(e):
        p.load_serial(e, p.ld, row[:], src_ap)

    @p.block.tensor
    def _(t):
        t.wait_ge(p.ld.h, p.ld.n)
        ins = None
        for j in range(nt):
            ins = t.transpose(pt[:, j:j + 1], row[:, j * w:(j + 1) * w],
                              p.consts["id_f32"][0:1, 0:1])
        p.tC.inc(ins)

    @p.block.vector
    def _(v):
        v.wait_ge(p.tC.h, p.tC.n)
        p.vC.inc(v.tensor_copy(out=cols[:], in_=pt[:, 0:nt]))

    for eng in ("tensor", "vector", "scalar", "gpsimd", "sync"):
        @getattr(p.block, eng)
        def _(e):
            e.wait_ge(p.tC.h, p.tC.n)
            e.wait_ge(p.vC.h, p.vC.n)
    lstack.close()
    return cols



def phase_l0_ffn(p, xsrc, xdst, W, stop_after=None, occ=0):
    def _stop(tag, stacks=()):
        if stop_after == tag:
            for st_ in stacks:
                st_.close()
            phase_end(p)
            return True
        return False
    phase_begin(p)
    HD = VD = 32
    NH = 16
    CHQ = 512
    FDIM = 2048
    FT = FDIM // 128

    if "l0w1" not in p.scratch:
        p.scratch["l0w1"] = p.nc.dram_tensor(
            "l0w1bf", [8, D, FDIM], BF16, kind="Internal").ap()
        p.scratch["l0w2"] = p.nc.dram_tensor(
            "l0w2bf", [8, FDIM, D], BF16, kind="Internal").ap()
    if occ == 0:
        for ex in range(8):
            emit_weights_to_bf16(p, W["ffn_w1"][ex], p.scratch["l0w1"][ex],
                                 D, FDIM, name=f"pw1{ex}")
            emit_weights_to_bf16(p, W["ffn_w2"][ex], p.scratch["l0w2"][ex],
                                 FDIM, D, name=f"pw2{ex}")
    w1scr = p.scratch["l0w1"]
    w2scr = p.scratch["l0w2"]

    wq = p.sbuf("wq", [128, DG * D], BF16)
    wk = p.sbuf("wk", [128, DG * 512], BF16)
    wv = p.sbuf("wv", [128, DG * 512], BF16)
    wo_sb = p.sbuf("wosb", [128, 8 * D], BF16)

    @p.block.gpsimd
    def _(e):
        p.gC.inc(e.memset(wo_sb[:], 0.0))
        for k in range(DG):
            p.load_serial(e, p.ldg, wq[:, k * D:(k + 1) * D],
                          W["ffn_q_w"][k * 128:(k + 1) * 128, :])
        for k in range(DG):
            p.load_serial(e, p.ldg, wk[:, k * 512:(k + 1) * 512],
                          W["ffn_k_w"][k * 128:(k + 1) * 128, :])
        for k in range(DG):
            p.load_serial(e, p.ldg, wv[:, k * 512:(k + 1) * 512],
                          W["ffn_v_w"][k * 128:(k + 1) * 128, :])
        e.wait_ge(p.gC.h, p.gC.n)
        for h in range(NH):
            p.load_serial(e, p.ldg,
                          wo_sb[(h % 2) * 64:(h % 2) * 64 + 32,
                                (h // 2) * D:(h // 2 + 1) * D],
                          W["ffn_o_w"][h * 32:(h + 1) * 32, :])

    qb32 = load_bias_cols_w(p, "qb", W["ffn_q_b"], D, 32)
    kb32 = load_bias_cols_w(p, "kb", W["ffn_k_b"], 512, 32)
    vb_b = load_bcast(p, "vbb", W["ffn_v_b"], 512)
    ob_cols = load_bias_cols(p, "ob", W["ffn_o_b"], D)
    b1cols = load_bias_cols(
        p, "b1", W["ffn_b1"].rearrange("e f -> (e f)"), 8 * FDIM)
    b2sb = p.sbuf("b2sb", [8, D], F32)

    @p.block.sync
    def _(e):
        p.load_serial(e, p.ld, b2sb[:], W["ffn_b2"])

    mlaT = p.sbuf("mlaT", [128, DG * S], BF16)
    gates = p.sbuf("gates", [128, 16 * 8], F32)
    gatesT = p.sbuf("gatesT", [8, S], F32)
    lnstats = [(p.sbuf(f"lm{b}", [128, 16], F32),
                p.sbuf(f"lr{b}", [128, 16], F32)) for b in range(2)]

    for b in range(B_PER_CORE):
        sub1 = ExitStack()

        def _s1(nm, shape, dtype):
            return sub1.enter_context(p.nc.sbuf_tensor(
                f"{nm}_{p.uid()}", shape, dtype))
        xnT = _s1("xnTb", [128, DG * S], BF16)
        vaug = _s1("vaug", [128, 16 * NH * (VD + 1)], BF16)
        attnT = _s1("attnT", [128, 8 * S], BF16)

        @p.block.gpsimd
        def _(e):
            e.wait_ge(p.gC.h, p.gC.n)
            va = vaug[:].rearrange("q (t h c) -> q t h c", t=16, h=NH)
            p.gC.inc(e.memset(va[:, :, :, VD:VD + 1], 1.0))

        @p.block.vector
        def _(e):
            e.wait_ge(p.vC.h, p.vC.n)
            p.vC.inc(e.memset(attnT[:], 0.0))
            e.wait_ge(p.vC.h, p.vC.n)
        ones_mark = p.gC.n

        xnT_, mcols, rcols, lnmarks = emit_ln_dmajor(
            p, lambda i, b=b: xsrc[(16 * b + i) * 128:(16 * b + i + 1) * 128, :],
            16, S, dst=xnT, name=f"l0f_ln{b}", stats=lnstats[b])

        vstk = ExitStack()
        psv = [vstk.enter_context(p.nc.psum_tensor(
            f"psv{b}{j}_{p.uid()}", [128, 512], F32)) for j in range(2)]
        psvmark = [0, 0]
        for mt in range(16):
            pv = mt % 2

            @p.block.tensor
            def _(e, b=b, mt=mt, pv=pv):
                if mt == 0:
                    e.wait_ge(p.gC.h, ones_mark)
                    e.wait_ge(p.ldg.h, p.ldg.n)
                e.wait_ge(p.sC.h, lnmarks[mt])
                e.wait_ge(p.vC.h, psvmark[pv])
                ins = None
                for k in range(DG):
                    ins = e.matmul(psv[pv][:],
                                   xnT[:, k * S + mt * 128:k * S + mt * 128 + 128],
                                   wv[:, k * 512:(k + 1) * 512],
                                   start=(k == 0), stop=(k == DG - 1))
                p.tC.inc(ins)
            vm = p.tC.n

            @p.block.vector
            def _(e, mt=mt, pv=pv, vm=vm):
                e.wait_ge(p.tC.h, vm)
                ov = vaug[:, mt * NH * (VD + 1):(mt + 1) * NH * (VD + 1)]
                ov = ov.rearrange("q (h c) -> q h c", h=NH)[:, :, 0:VD]
                iv = psv[pv][:].rearrange("q (h d) -> q h d", h=NH)
                bv = vb_b[:, 0:512].rearrange("q (h d) -> q h d", h=NH)
                psvmark[pv] = p.vC.inc(e.tensor_add(out=ov, in0=iv, in1=bv))

        @p.block.tensor
        def _(e):
            e.wait_ge(p.vC.h, p.vC.n)
        vstk.close()
        if _stop("v", (sub1,)):
            return

        hstk = ExitStack()

        def _hs(nm, shape, dtype):
            return hstk.enter_context(p.nc.sbuf_tensor(
                f"{nm}_{p.uid()}", shape, dtype))
        qT = _hs("qTh", [HD, S], BF16)
        kT = _hs("kTh", [HD, S], BF16)
        qsq = _hs("qsq", [HD, S], BF16)
        cc = _hs("cc", [1, 8], F32)
        negc = _hs("negc", [128, 1], F32)
        pexp = [_hs(f"pexp{j}", [128, 2 * CHQ], BF16) for j in range(2)]
        tmps = [_hs(f"tmps{j}", [128, 2 * CHQ], F32) for j in range(2)]
        recb = _hs("recb", [VD, CHQ], F32)
        psA = [hstk.enter_context(p.nc.psum_tensor(
            f"psA{b}{j}_{p.uid()}", [128, 2 * CHQ], F32)) for j in range(2)]
        psO = [hstk.enter_context(p.nc.psum_tensor(
            f"psO{b}{j}_{p.uid()}", [VD + 1, CHQ], F32)) for j in range(2)]
        psR = hstk.enter_context(p.nc.psum_tensor(
            f"psR{b}_{p.uid()}", [128, CHQ], F32))
        psq = hstk.enter_context(p.nc.psum_tensor(
            f"psq{b}_{p.uid()}", [HD, CHQ], F32))
        pqmark = [0]

        def proj_qk(h, qT_, kT_, b=b):
            for which in (0, 1):
                dstT = qT_ if which == 0 else kT_
                for ci4 in range(S // 512):
                    @p.block.tensor
                    def _(e, h=h, which=which, ci4=ci4):
                        e.wait_ge(p.sC.h, pqmark[0])
                        ins = None
                        for k in range(DG):
                            if which == 0:
                                lh = wq[:, k * D + h * 64:k * D + h * 64 + 32]
                            else:
                                lh = wk[:, k * 512 + h * 32:
                                        k * 512 + h * 32 + 32]
                            ins = e.matmul(psq[:, 0:512], lh,
                                           xnT[:, k * S + ci4 * 512:
                                               k * S + ci4 * 512 + 512],
                                           start=(k == 0), stop=(k == DG - 1))
                        p.tC.inc(ins)
                    pm = p.tC.n

                    @p.block.scalar
                    def _(e, h=h, which=which, ci4=ci4, pm=pm):
                        e.wait_ge(p.tC.h, pm)
                        bcol = (qb32[:, 2 * h:2 * h + 1] if which == 0
                                else kb32[:, h:h + 1])
                        dstT_ = qT_ if which == 0 else kT_
                        pqmark[0] = p.sC.inc(e.add(
                            out=dstT_[0:HD, ci4 * 512:(ci4 + 1) * 512],
                            in_=psq[:, 0:512], add=bcol))

        def vslice(kt, h):
            base = kt * NH * (VD + 1) + h * (VD + 1)
            return vaug[:, base:base + VD + 1]

        def attn_slice(h, c0):
            return attnT[(h % 2) * 64:(h % 2) * 64 + VD,
                         (h // 2) * S + c0:(h // 2) * S + c0 + CHQ]

        cfg = dict(HD=HD, VD=VD, NH=NH, S=S, CHQ=CHQ, qT=qT, kT=kT, qsq=qsq,
                   cc=cc, negc=negc, psA=psA, psO=psO, psR=psR, pexp=pexp,
                   tmps=tmps, recb=recb, proj_qk=proj_qk, vslice=vslice,
                   bias_slice=lambda h, kt, c0: None, bias_ready=None,
                   bias_wait=None, bias_slice_any=False,
                   mask=lambda kt, c0: None, extra_c=None,
                   scale=1.0 / math.sqrt(HD), attn_slice=attn_slice)
        emit_mha_batch(p, b, cfg)

        for eng in ("tensor", "vector", "scalar", "gpsimd", "sync"):
            @getattr(p.block, eng)
            def _(e):
                e.wait_ge(p.vC.h, p.vC.n)
        hstk.close()
        if _stop("h", (sub1,)):
            return

        mstk = ExitStack()
        psD = [mstk.enter_context(p.nc.psum_tensor(
            f"psD{b}{j}_{p.uid()}", [128, 512], F32)) for j in range(2)]
        psDmark = [0, 0]
        for dt in range(DG):
            for ch in range(4):
                pd = (dt * 4 + ch) % 2

                @p.block.tensor
                def _(e, b=b, dt=dt, ch=ch, pd=pd):
                    e.wait_ge(p.sC.h, psDmark[pd])
                    ins = None
                    for blk in range(8):
                        ins = e.matmul(
                            psD[pd][:],
                            wo_sb[:, blk * D + dt * 128:blk * D + dt * 128 + 128],
                            attnT[:, blk * S + ch * 512:blk * S + ch * 512 + 512],
                            start=(blk == 0), stop=(blk == 7))
                    p.tC.inc(ins)
                dm = p.tC.n

                @p.block.scalar
                def _(e, dt=dt, ch=ch, pd=pd, dm=dm):
                    e.wait_ge(p.tC.h, dm)
                    psDmark[pd] = p.sC.inc(e.add(
                        out=mlaT[:, dt * S + ch * 512:dt * S + ch * 512 + 512],
                        in_=psD[pd][:], add=ob_cols[:, dt:dt + 1]))

        for eng in ("tensor", "vector", "scalar", "gpsimd", "sync"):
            @getattr(p.block, eng)
            def _(e):
                e.wait_ge(p.sC.h, p.sC.n)
        mstk.close()
        sub1.close()
        if _stop("m"):
            return

        emit_router_gates(p, mlaT, W["ffn_router_w"], W["ffn_router_b"],
                          gates, gatesT, use_eps=True, name=f"rt{b}",
                          ntiles=16,
                          limit={"r0": "loads", "r1": "gates", "r2": "mm",
                                 "r3": "exp"}.get(stop_after))
        if _stop("r") or _stop("r0") or _stop("r1") or _stop("r2") \
                or _stop("r3"):
            return

        xrb = None
        fstk = ExitStack()
        xrb = [fstk.enter_context(p.nc.sbuf_tensor(
            f"xrb{b}{j}_{p.uid()}", [128, D], F32)) for j in range(2)]
        xov = [fstk.enter_context(p.nc.sbuf_tensor(
            f"xov{b}{j}_{p.uid()}", [128, D], F32)) for j in range(2)]
        tmpa = fstk.enter_context(p.nc.sbuf_tensor(
            f"tmpa{b}_{p.uid()}", [128, D], F32))
        xrsem = [p.r[9 + j] for j in range(2)]
        xrmark = [0, 0]
        stsem = [p.r[11 + j] for j in range(2)]
        mc, rc = lnstats[b]

        def final_fn(mtl, acc_ap, b=b, mc=mc, rc=rc):
            tile = 16 * b + mtl
            xs = mtl % 2

            @p.block.sync
            def _(e, tile=tile, xs=xs):
                e.wait_ge(p.vC.h, xrmark[xs])
                e.wait_ge(stsem[xs].h, stsem[xs].n)
                xrsem[xs].inc(e.dma_start(
                    out=xrb[xs][:],
                    in_=xsrc[tile * 128:(tile + 1) * 128, :]), 16)

            @p.block.vector
            def _(e, tile=tile, mtl=mtl, xs=xs):
                def step(ins):
                    p.vC.inc(ins)
                    e.wait_ge(p.vC.h, p.vC.n)
                e.wait_ge(xrsem[xs].h, xrsem[xs].n)
                e.wait_ge(p.vC.h, p.vC.n)
                step(e.tensor_scalar(out=tmpa[:], in0=xrb[xs][:],
                                     scalar1=mc[:, mtl:mtl + 1],
                                     scalar2=rc[:, mtl:mtl + 1],
                                     op0=ALU.subtract, op1=ALU.mult))
                step(e.scalar_tensor_tensor(out=tmpa[:], in0=tmpa[:],
                                            scalar=0.5, in1=xrb[xs][:],
                                            op0=ALU.mult, op1=ALU.add))
                xrmark[xs] = p.vC.inc(e.scalar_tensor_tensor(
                    out=xov[xs][:], in0=acc_ap, scalar=0.15, in1=tmpa[:],
                    op0=ALU.mult, op1=ALU.add))
                e.wait_ge(p.vC.h, p.vC.n)

            @p.block.sync
            def _(e, tile=tile, xs=xs):
                e.wait_ge(p.vC.h, p.vC.n)
                stsem[xs].inc(e.dma_start(
                    out=xdst[tile * 128:(tile + 1) * 128, :],
                    in_=xov[xs][:]), 16)

        emit_moe(p, mlaT, gates, gatesT,
                 lambda ex, dg, fb: w1scr[
                     ex, dg * 128:(dg + 1) * 128, fb * 512:(fb + 1) * 512],
                 b1cols,
                 lambda ex, ft: w2scr[ex, ft * 128:(ft + 1) * 128, :],
                 b2sb, FDIM, AF.Gelu, final_fn, name=f"moe{b}",
                 tile0=0, ntiles=16)

        @p.block.sync
        def _(e):
            e.wait_ge(stsem[0].h, stsem[0].n)
            e.wait_ge(stsem[1].h, stsem[1].n)
        for eng in ("tensor", "vector", "scalar", "gpsimd"):
            @getattr(p.block, eng)
            def _(e):
                e.wait_ge(stsem[0].h, stsem[0].n)
                e.wait_ge(stsem[1].h, stsem[1].n)
        fstk.close()
    phase_end(p)





def phase_l3_ffn(p, xsrc, xdst, W, occ=0):
    phase_begin(p)
    HD = VD = 64
    NH = 16
    CHQ = 512
    FDIM = 4096
    FT = FDIM // 128

    x2scr = p.nc.dram_tensor(f"x2scr_{p.uid()}", [T, D], F32,
                             kind="Internal").ap()
    if "l3w1" not in p.scratch:
        p.scratch["l3w1"] = p.nc.dram_tensor(
            "l3w1bf", [8, D, FDIM], BF16, kind="Internal").ap()
        p.scratch["l3w2"] = p.nc.dram_tensor(
            "l3w2bf", [8, FDIM, D], BF16, kind="Internal").ap()
    if occ == 0:
        for ex in range(8):
            emit_weights_to_bf16(p, W["ffn_W1"][ex], p.scratch["l3w1"][ex],
                                 D, FDIM, name=f"pw1{ex}")
            emit_weights_to_bf16(p, W["ffn_W2"][ex], p.scratch["l3w2"][ex],
                                 FDIM, D, name=f"pw2{ex}")
    w1scr = p.scratch["l3w1"]
    w2scr = p.scratch["l3w2"]

    g1b = load_bcast(p, "g1b", W["ffn_ln1_g"], D)
    b1b = load_bcast(p, "b1b", W["ffn_ln1_b"], D)
    g2b = load_bcast(p, "g2b", W["ffn_ln2_g"], D)
    b2b_ = load_bcast(p, "b2b", W["ffn_ln2_b"], D)
    b1cols = load_bias_cols(
        p, "b1c", W["ffn_b1"].rearrange("e f -> (e f)"), 8 * FDIM)
    b2sb = p.sbuf("b2sb", [8, D], F32)

    @p.block.sync
    def _(e):
        p.load_serial(e, p.ld, b2sb[:], W["ffn_b2"])

    wv = p.sbuf("wv", [128, DG * D], BF16)
    wo = p.sbuf("wo", [128, DG * D], BF16)

    @p.block.gpsimd
    def _(e):
        for k in range(DG):
            p.load_serial(e, p.ldg, wv[:, k * D:(k + 1) * D],
                          W["ffn_Wv"][k * 128:(k + 1) * 128, :])
        for k in range(DG):
            p.load_serial(e, p.ldg, wo[:, k * D:(k + 1) * D],
                          W["ffn_Wo"][k * 128:(k + 1) * 128, :])

    lnstats = [(p.sbuf(f"lm{b}", [128, 16], F32),
                p.sbuf(f"lr{b}", [128, 16], F32)) for b in range(2)]
    qkw = [p.sbuf(f"qkw{j}", [128, DG * 128], BF16) for j in range(2)]
    qksem = [p.r[15 + j] for j in range(2)]
    qkmark = [0, 0]

    for b in range(B_PER_CORE):
        sub1 = ExitStack()

        def _s1(nm, shape, dtype):
            return sub1.enter_context(p.nc.sbuf_tensor(
                f"{nm}_{p.uid()}", shape, dtype))
        hT = _s1("hT", [128, DG * S], BF16)
        vaug = _s1("vaug", [128, 16 * NH * (HD + 1)], BF16)
        attnT = _s1("attnT", [128, DG * S], BF16)

        @p.block.gpsimd
        def _(e):
            e.wait_ge(p.gC.h, p.gC.n)
            va = vaug[:].rearrange("q (t h c) -> q t h c", t=16, h=NH)
            p.gC.inc(e.memset(va[:, :, :, HD:HD + 1], 1.0))
        ones_mark = p.gC.n

        hT_, mcols, rcols, lnmarks = emit_ln_dmajor(
            p, lambda i, b=b: xsrc[(16 * b + i) * 128:(16 * b + i + 1) * 128, :],
            16, S, dst=hT, name=f"l3f_ln{b}", double=True,
            affine=(g1b, b1b), stats=lnstats[b])

        vstk = ExitStack()
        psv = [vstk.enter_context(p.nc.psum_tensor(
            f"psv{b}{j}_{p.uid()}", [128, D], F32)) for j in range(2)]
        psvmark = [0, 0]
        for mt in range(16):
            pv = mt % 2

            @p.block.tensor
            def _(e, b=b, mt=mt, pv=pv):
                if mt == 0:
                    e.wait_ge(p.gC.h, ones_mark)
                    e.wait_ge(p.ldg.h, p.ldg.n)
                e.wait_ge(p.sC.h, lnmarks[mt])
                e.wait_ge(p.vC.h, psvmark[pv])
                ins = None
                for k in range(DG):
                    for n0 in (0, 512):
                        ins = e.matmul(
                            psv[pv][:, n0:n0 + 512],
                            hT[:, k * S + mt * 128:k * S + mt * 128 + 128],
                            wv[:, k * D + n0:k * D + n0 + 512],
                            start=(k == 0), stop=(k == DG - 1))
                p.tC.inc(ins)
            vm = p.tC.n

            @p.block.vector
            def _(e, mt=mt, pv=pv, vm=vm):
                e.wait_ge(p.tC.h, vm)
                ov = vaug[:, mt * NH * (HD + 1):(mt + 1) * NH * (HD + 1)]
                ov = ov.rearrange("q (h c) -> q h c", h=NH)[:, :, 0:HD]
                iv = psv[pv][:].rearrange("q (h d) -> q h d", h=NH)
                psvmark[pv] = p.vC.inc(e.tensor_copy(out=ov, in_=iv))

        @p.block.tensor
        def _(e):
            e.wait_ge(p.vC.h, p.vC.n)
        vstk.close()

        hstk = ExitStack()

        def _hs(nm, shape, dtype):
            return hstk.enter_context(p.nc.sbuf_tensor(
                f"{nm}_{p.uid()}", shape, dtype))
        qT = _hs("qTh", [HD, S], BF16)
        kT = _hs("kTh", [HD, S], BF16)
        qsq = _hs("qsq", [HD, S], BF16)
        cc = _hs("cc", [1, 8], F32)
        negc = _hs("negc", [128, 1], F32)
        pexp = [_hs(f"pexp{j}", [128, 2 * CHQ], BF16) for j in range(2)]
        tmps = [_hs(f"tmps{j}", [128, 2 * CHQ], F32) for j in range(2)]
        recb = _hs("recb", [HD, CHQ], F32)
        psA = [hstk.enter_context(p.nc.psum_tensor(
            f"psA{b}{j}_{p.uid()}", [128, 2 * CHQ], F32)) for j in range(2)]
        psO = [hstk.enter_context(p.nc.psum_tensor(
            f"psO{b}{j}_{p.uid()}", [HD + 1, CHQ], F32)) for j in range(2)]
        psR = hstk.enter_context(p.nc.psum_tensor(
            f"psR{b}_{p.uid()}", [128, CHQ], F32))
        psq = hstk.enter_context(p.nc.psum_tensor(
            f"psq{b}_{p.uid()}", [HD, CHQ], F32))
        pqmark = [0]

        def qk_stream(h):
            sl = h % 2

            @p.block.gpsimd
            def _(e, h=h, sl=sl):
                e.wait_ge(p.tC.h, qkmark[sl])
                for k in range(DG):
                    qksem[sl].inc(e.dma_start(
                        out=qkw[sl][:, k * 128:k * 128 + 64],
                        in_=W["ffn_Wq"][k * 128:(k + 1) * 128,
                                        h * HD:h * HD + HD]), 16)
                    qksem[sl].inc(e.dma_start(
                        out=qkw[sl][:, k * 128 + 64:k * 128 + 128],
                        in_=W["ffn_Wk"][k * 128:(k + 1) * 128,
                                        h * HD:h * HD + HD]), 16)

        def proj_qk(h, qT_, kT_, b=b):
            qk_stream(h)
            sl = h % 2
            qkn = qksem[sl].n
            for which in (0, 1):
                dstT = qT_ if which == 0 else kT_
                for ci4 in range(S // 512):
                    @p.block.tensor
                    def _(e, h=h, sl=sl, which=which, ci4=ci4, qkn=qkn):
                        e.wait_ge(qksem[sl].h, qkn)
                        e.wait_ge(p.sC.h, pqmark[0])
                        ins = None
                        for k in range(DG):
                            ins = e.matmul(
                                psq[:, 0:512],
                                qkw[sl][:, k * 128 + which * 64:
                                        k * 128 + which * 64 + 64],
                                hT[:, k * S + ci4 * 512:k * S + ci4 * 512 + 512],
                                start=(k == 0), stop=(k == DG - 1))
                        qkmark[sl] = p.tC.inc(ins)
                    pm = p.tC.n

                    @p.block.scalar
                    def _(e, dstT=dstT, ci4=ci4, pm=pm):
                        e.wait_ge(p.tC.h, pm)
                        pqmark[0] = p.sC.inc(e.copy(
                            out=dstT[0:HD, ci4 * 512:(ci4 + 1) * 512],
                            in_=psq[:, 0:512]))

        def vslice(kt, h):
            base = kt * NH * (HD + 1) + h * (HD + 1)
            return vaug[:, base:base + HD + 1]

        def attn_slice(h, c0):
            return attnT[(h % 2) * 64:(h % 2) * 64 + 64,
                         (h // 2) * S + c0:(h // 2) * S + c0 + CHQ]

        cfg = dict(HD=HD, VD=VD, NH=NH, S=S, CHQ=CHQ, qT=qT, kT=kT, qsq=qsq,
                   cc=cc, negc=negc, psA=psA, psO=psO, psR=psR, pexp=pexp,
                   tmps=tmps, recb=recb, proj_qk=proj_qk, vslice=vslice,
                   bias_slice=lambda h, kt, c0: None, bias_ready=None,
                   bias_wait=None, bias_slice_any=False,
                   mask=lambda kt, c0: None, extra_c=None,
                   scale=1.0 / math.sqrt(HD), attn_slice=attn_slice)
        emit_mha_batch(p, b, cfg)

        for eng in ("tensor", "vector", "scalar", "gpsimd", "sync"):
            @getattr(p.block, eng)
            def _(e):
                e.wait_ge(p.vC.h, p.vC.n)
        hstk.close()

        wstk = ExitStack()

        def _ws(nm, shape, dtype):
            return wstk.enter_context(p.nc.sbuf_tensor(
                f"{nm}_{p.uid()}", shape, dtype))
        xrb = [_ws(f"xrb{j}", [128, D], F32) for j in range(2)]
        xov = [_ws(f"xov{j}", [128, D], F32) for j in range(2)]
        tmpw = _ws("tmpw", [128, D], F32)
        xrsem = [p.r[9 + j] for j in range(2)]
        xrmark = [0, 0]
        stsem = [p.r[11 + j] for j in range(2)]
        psres = [wstk.enter_context(p.nc.psum_tensor(
            f"psres{b}{j}_{p.uid()}", [128, D], F32)) for j in range(2)]
        resmark = [0, 0]
        mc, rc = lnstats[b]
        for mt in range(16):
            tile = 16 * b + mt
            pr = mt % 2
            xs = mt % 2

            @p.block.sync
            def _(e, tile=tile, xs=xs):
                e.wait_ge(p.vC.h, xrmark[xs])
                e.wait_ge(stsem[xs].h, stsem[xs].n)
                xrsem[xs].inc(e.dma_start(
                    out=xrb[xs][:],
                    in_=xsrc[tile * 128:(tile + 1) * 128, :]), 16)

            @p.block.tensor
            def _(e, b=b, mt=mt, pr=pr):
                e.wait_ge(p.vC.h, resmark[pr])
                ins = None
                for dt in range(DG):
                    for n0 in (0, 512):
                        ins = e.matmul(
                            psres[pr][:, n0:n0 + 512],
                            attnT[:, dt * S + mt * 128:dt * S + mt * 128 + 128],
                            wo[:, dt * D + n0:dt * D + n0 + 512],
                            start=(dt == 0), stop=(dt == DG - 1))
                p.tC.inc(ins)
            wm = p.tC.n

            @p.block.vector
            def _(e, mt=mt, xs=xs, pr=pr, wm=wm):
                def step(ins):
                    p.vC.inc(ins)
                    e.wait_ge(p.vC.h, p.vC.n)
                e.wait_ge(p.tC.h, wm)
                e.wait_ge(xrsem[xs].h, xrsem[xs].n)
                step(e.tensor_scalar(out=tmpw[:], in0=xrb[xs][:],
                                     scalar1=mc[:, mt:mt + 1],
                                     scalar2=rc[:, mt:mt + 1],
                                     op0=ALU.subtract, op1=ALU.mult))
                resmark[pr] = p.vC.inc(e.tensor_add(
                    out=xov[xs][:], in0=psres[pr][:], in1=tmpw[:]))
                e.wait_ge(p.vC.h, p.vC.n)
                xrmark[xs] = p.vC.n

            @p.block.sync
            def _(e, tile=tile, xs=xs):
                e.wait_ge(p.vC.h, p.vC.n)
                stsem[xs].inc(e.dma_start(
                    out=x2scr[tile * 128:(tile + 1) * 128, :],
                    in_=xov[xs][:]), 16)

        @p.block.sync
        def _(e):
            e.wait_ge(stsem[0].h, stsem[0].n)
            e.wait_ge(stsem[1].h, stsem[1].n)
        for eng in ("tensor", "vector", "scalar", "gpsimd"):
            @getattr(p.block, eng)
            def _(e):
                e.wait_ge(stsem[0].h, stsem[0].n)
                e.wait_ge(stsem[1].h, stsem[1].n)
        wstk.close()
        sub1.close()

        bstk = ExitStack()
        mT = bstk.enter_context(p.nc.sbuf_tensor(
            f"mT{b}_{p.uid()}", [128, DG * S], BF16))
        gates = bstk.enter_context(p.nc.sbuf_tensor(
            f"gates{b}_{p.uid()}", [128, 16 * 8], F32))
        gatesT = bstk.enter_context(p.nc.sbuf_tensor(
            f"gatesT{b}_{p.uid()}", [8, S], F32))
        m2stats = (bstk.enter_context(p.nc.sbuf_tensor(
            f"m2m{b}_{p.uid()}", [128, 16], F32)),
                   bstk.enter_context(p.nc.sbuf_tensor(
            f"m2r{b}_{p.uid()}", [128, 16], F32)))
        mT_, _, _, _ = emit_ln_dmajor(
            p, lambda i, b=b: x2scr[(16 * b + i) * 128:(16 * b + i + 1) * 128, :],
            16, S, dst=mT, name=f"l3f_ln2{b}", affine=(g2b, b2b_),
            stats=m2stats)

        emit_router_gates(p, mT, W["ffn_Wg"], None, gates, gatesT,
                          use_eps=False, name=f"rt{b}", ntiles=16)

        fstk = ExitStack()
        xrb2 = [fstk.enter_context(p.nc.sbuf_tensor(
            f"xrb2{b}{j}_{p.uid()}", [128, D], F32)) for j in range(2)]
        x2b = [fstk.enter_context(p.nc.sbuf_tensor(
            f"x2b{b}{j}_{p.uid()}", [128, D], F32)) for j in range(2)]
        xov2 = [fstk.enter_context(p.nc.sbuf_tensor(
            f"xov2{b}{j}_{p.uid()}", [128, D], F32)) for j in range(2)]
        tmpa = fstk.enter_context(p.nc.sbuf_tensor(
            f"tmpa{b}_{p.uid()}", [128, D], F32))
        xr2sem = [p.r[13 + j] for j in range(2)]
        x2sem = [p.r[17 + j] for j in range(2)]
        frmark = [0, 0]
        stsem2 = [p.r[11 + j] for j in range(2)]

        def final_fn(mtl, acc_ap, b=b):
            tile = 16 * b + mtl
            xs = mtl % 2

            @p.block.sync
            def _(e, tile=tile, xs=xs):
                e.wait_ge(p.vC.h, frmark[xs])
                e.wait_ge(stsem2[xs].h, stsem2[xs].n)
                xr2sem[xs].inc(e.dma_start(
                    out=xrb2[xs][:],
                    in_=xsrc[tile * 128:(tile + 1) * 128, :]), 16)
                x2sem[xs].inc(e.dma_start(
                    out=x2b[xs][:],
                    in_=x2scr[tile * 128:(tile + 1) * 128, :]), 16)

            @p.block.vector
            def _(e, tile=tile, xs=xs):
                def step(ins):
                    p.vC.inc(ins)
                    e.wait_ge(p.vC.h, p.vC.n)
                e.wait_ge(xr2sem[xs].h, xr2sem[xs].n)
                e.wait_ge(x2sem[xs].h, x2sem[xs].n)
                e.wait_ge(p.vC.h, p.vC.n)
                step(e.scalar_tensor_tensor(out=tmpa[:], in0=x2b[xs][:],
                                            scalar=0.5, in1=xrb2[xs][:],
                                            op0=ALU.mult, op1=ALU.add))
                frmark[xs] = p.vC.inc(e.scalar_tensor_tensor(
                    out=xov2[xs][:], in0=acc_ap, scalar=0.5, in1=tmpa[:],
                    op0=ALU.mult, op1=ALU.add))
                e.wait_ge(p.vC.h, p.vC.n)

            @p.block.sync
            def _(e, tile=tile, xs=xs):
                e.wait_ge(p.vC.h, p.vC.n)
                stsem2[xs].inc(e.dma_start(
                    out=xdst[tile * 128:(tile + 1) * 128, :],
                    in_=xov2[xs][:]), 16)

        emit_moe(p, mT, gates, gatesT,
                 lambda ex, dg, fb: w1scr[
                     ex, dg * 128:(dg + 1) * 128, fb * 512:(fb + 1) * 512],
                 b1cols,
                 lambda ex, ft: w2scr[ex, ft * 128:(ft + 1) * 128, :],
                 b2sb, FDIM, AF.Gelu_apprx_tanh, final_fn, name=f"moe{b}",
                 tile0=0, ntiles=16)

        @p.block.sync
        def _(e):
            e.wait_ge(stsem2[0].h, stsem2[0].n)
            e.wait_ge(stsem2[1].h, stsem2[1].n)
        for eng in ("tensor", "vector", "scalar", "gpsimd"):
            @getattr(p.block, eng)
            def _(e):
                e.wait_ge(stsem2[0].h, stsem2[0].n)
                e.wait_ge(stsem2[1].h, stsem2[1].n)
        fstk.close()
        bstk.close()
    phase_end(p)





def phase_l2_ffn(p, xsrc, xdst, W, occ=0):
    phase_begin(p)
    FDIM = 4096
    FT = FDIM // 128

    if "l2w1T" not in p.scratch:
        p.scratch["l2w1T"] = p.nc.dram_tensor(
            "l2w1T", [8, D, FDIM], BF16, kind="Internal").ap()
        p.scratch["l2w2T"] = p.nc.dram_tensor(
            "l2w2T", [8, FDIM, D], BF16, kind="Internal").ap()
    w1T_scr = p.scratch["l2w1T"]
    w2T_scr = p.scratch["l2w2T"]

    if occ == 0:
        tstk = ExitStack()
        ttmp = [tstk.enter_context(p.nc.sbuf_tensor(
            f"ttmp{j}_{p.uid()}", [128, FDIM], BF16)) for j in range(2)]
        tout = [tstk.enter_context(p.nc.sbuf_tensor(
            f"tout{j}_{p.uid()}", [128, 1024], BF16)) for j in range(2)]
        ptp = [tstk.enter_context(p.nc.psum_tensor(
            f"ptp{j}_{p.uid()}", [128, 1024], BF16)) for j in range(2)]
        tsem = [p.r[19 + j] for j in range(2)]
        osem = [p.r[21 + j] for j in range(2)]
        tmark = [0, 0]
        emark2 = [0, 0]
        jobs = []
        for ex in range(8):
            for i in range(FDIM // 128):
                jobs.append((W["ffn_w1"], w1T_scr, ex, i, FDIM, D))
            for i in range(D // 128):
                jobs.append((W["ffn_w2"], w2T_scr, ex, i, D, FDIM))
        for jn, (srcw, dstw, ex, i, A, B) in enumerate(jobs):
            sl = jn % 2

            @p.block.gpsimd
            def _(e, srcw=srcw, ex=ex, i=i, B=B, sl=sl, jn=jn):
                if jn >= 2:
                    e.wait_ge(p.tC.h, tmark[sl])
                for bb in range(B // 1024):
                    tsem[sl].inc(e.dma_start(
                        out=ttmp[sl][:, bb * 1024:(bb + 1) * 1024],
                        in_=srcw[ex, i * 128:(i + 1) * 128,
                                 bb * 1024:(bb + 1) * 1024]), 16)

            for bb in range(B // 1024):
                @p.block.tensor
                def _(e, sl=sl, bb=bb, jn=jn, B=B):
                    e.wait_ge(tsem[sl].h, tsem[sl].n)
                    if jn >= 2 or bb > 0:
                        e.wait_ge(p.sC.h, emark2[sl])
                    ins = None
                    for j in range(8):
                        ins = e.transpose(
                            ptp[sl][:, j * 128:(j + 1) * 128],
                            ttmp[sl][:, bb * 1024 + j * 128:
                                     bb * 1024 + j * 128 + 128],
                            p.consts["id_bf"][:])
                    tm = p.tC.inc(ins)
                    if bb == B // 1024 - 1:
                        tmark[sl] = tm
                tm2 = p.tC.n

                @p.block.scalar
                def _(e, sl=sl, tm2=tm2, jn=jn):
                    e.wait_ge(p.tC.h, tm2)
                    if jn >= 2:
                        e.wait_ge(osem[sl].h, osem[sl].n)
                    emark2[sl] = p.sC.inc(e.copy(out=tout[sl][:],
                                                 in_=ptp[sl][:]))
                sm = p.sC.n

                @p.block.sync
                def _(e, dstw=dstw, ex=ex, i=i, bb=bb, sl=sl, sm=sm):
                    e.wait_ge(p.sC.h, sm)
                    for j in range(8):
                        osem[sl].inc(e.dma_start(
                            out=dstw[ex, bb * 1024 + j * 128:
                                     bb * 1024 + j * 128 + 128,
                                     i * 128:(i + 1) * 128],
                            in_=tout[sl][:, j * 128:(j + 1) * 128]), 16)

        @p.block.sync
        def _(e):
            e.wait_ge(osem[0].h, osem[0].n)
            e.wait_ge(osem[1].h, osem[1].n)
        for eng in ("tensor", "vector", "scalar", "gpsimd"):
            @getattr(p.block, eng)
            def _(e):
                e.wait_ge(osem[0].h, osem[0].n)
                e.wait_ge(osem[1].h, osem[1].n)
        tstk.close()

    centT = p.sbuf("centT", [128, DG * 8], BF16)
    cstk = ExitStack()
    csb = cstk.enter_context(p.nc.sbuf_tensor(
        f"csb_{p.uid()}", [8, D], BF16))
    pct = cstk.enter_context(p.nc.psum_tensor(
        f"pct_{p.uid()}", [128, DG * 8], BF16))

    @p.block.gpsimd
    def _(e):
        p.load_serial(e, p.ldg, csb[:], W["ffn_centroids"])

    @p.block.tensor
    def _(e):
        e.wait_ge(p.ldg.h, p.ldg.n)
        ins = None
        for k in range(DG):
            ins = e.transpose(pct[:, k * 8:(k + 1) * 8],
                              csb[:, k * 128:(k + 1) * 128],
                              p.consts["id_bf"][0:8, 0:8])
        p.tC.inc(ins)

    @p.block.vector
    def _(e):
        e.wait_ge(p.tC.h, p.tC.n)
        p.vC.inc(e.tensor_copy(out=centT[:], in_=pct[:]))
        e.wait_ge(p.vC.h, p.vC.n)
    for eng in ("tensor", "scalar", "gpsimd", "sync"):
        @getattr(p.block, eng)
        def _(e):
            e.wait_ge(p.vC.h, p.vC.n)
    cstk.close()

    gcols = load_bias_cols(p, "gc", W["ffn_ln_g"].rearrange("e d -> (e d)"),
                           8 * D)
    bcols = load_bias_cols(p, "bc", W["ffn_ln_b"].rearrange("e d -> (e d)"),
                           8 * D)
    b1cols = load_bias_cols(
        p, "b1c", W["ffn_b1"].rearrange("e f -> (e f)"), 8 * FDIM)
    b2sb = p.sbuf("b2sb", [8, D], F32)

    @p.block.sync
    def _(e):
        p.load_serial(e, p.ld, b2sb[:], W["ffn_b2"])

    lnstats = [(p.sbuf(f"lm{b}", [128, 16], F32),
                p.sbuf(f"lr{b}", [128, 16], F32)) for b in range(2)]

    for b in range(B_PER_CORE):
        bstk = ExitStack()
        xn2T = bstk.enter_context(p.nc.sbuf_tensor(
            f"xn2T{b}_{p.uid()}", [128, DG * S], BF16))
        xnT = bstk.enter_context(p.nc.sbuf_tensor(
            f"xnT{b}_{p.uid()}", [128, DG * S], BF16))
        gates = bstk.enter_context(p.nc.sbuf_tensor(
            f"gates{b}_{p.uid()}", [128, 16 * 8], F32))
        gatesT = bstk.enter_context(p.nc.sbuf_tensor(
            f"gatesT{b}_{p.uid()}", [8, S], F32))
        xeT = bstk.enter_context(p.nc.sbuf_tensor(
            f"xeT{b}_{p.uid()}", [128, DG * 512], BF16))

        _, mcols, rcols, lnmarks = emit_ln_dmajor(
            p, lambda i, b=b: xsrc[(16 * b + i) * 128:(16 * b + i + 1) * 128, :],
            16, S, dst=xn2T, name=f"l2f_ln{b}", double=True,
            stats=lnstats[b], dst2=xnT)

        gstk = ExitStack()
        psg = [gstk.enter_context(p.nc.psum_tensor(
            f"psg{b}{j}_{p.uid()}", [128, 8], F32)) for j in range(2)]
        psgt = gstk.enter_context(p.nc.psum_tensor(
            f"psgt{b}_{p.uid()}", [8, S], F32))
        gscr = gstk.enter_context(p.nc.sbuf_tensor(
            f"gscr{b}_{p.uid()}", [128, 2], F32))
        gevm = [0, 0]
        for mt in range(16):
            pg = mt % 2
            gg = gates[:, mt * 8:(mt + 1) * 8]

            @p.block.tensor
            def _(e, mt=mt, pg=pg):
                if mt == 0:
                    e.wait_ge(p.vC.h, p.vC.n)
                e.wait_ge(p.sC.h, lnmarks[mt])
                if mt >= 2:
                    e.wait_ge(p.vC.h, gevm[pg])
                ins = None
                for k in range(DG):
                    ins = e.matmul(psg[pg][:],
                                   xnT[:, k * S + mt * 128:k * S + mt * 128 + 128],
                                   centT[:, k * 8:(k + 1) * 8],
                                   start=(k == 0), stop=(k == DG - 1))
                p.tC.inc(ins)
            gm = p.tC.n

            @p.block.vector
            def _(e, mt=mt, pg=pg, gg=gg, gm=gm):
                def step(ins):
                    p.vC.inc(ins)
                    e.wait_ge(p.vC.h, p.vC.n)
                e.wait_ge(p.tC.h, gm)
                step(e.tensor_copy(out=gg, in_=psg[pg][:]))
                step(e.reduce_max(out=gscr[:, 0:1], in_=gg, axis=AX.X))
            vm = p.vC.n

            @p.block.scalar
            def _(e, vm=vm):
                e.wait_ge(p.vC.h, vm)
                p.sC.inc(e.activation(out=gscr[:, 1:2], in_=gscr[:, 0:1],
                                      func=AF.Sigmoid))
            sm = p.sC.n

            @p.block.vector
            def _(e, mt=mt, pg=pg, gg=gg, sm=sm):
                def step(ins):
                    p.vC.inc(ins)
                    e.wait_ge(p.vC.h, p.vC.n)
                e.wait_ge(p.sC.h, sm)
                step(e.tensor_single_scalar(out=gg, in_=gg,
                                            scalar=gscr[:, 0:1],
                                            op=ALU.is_ge))
                gevm[pg] = p.vC.inc(e.tensor_scalar_mul(
                    out=gg, in0=gg, scalar1=gscr[:, 1:2]))
                e.wait_ge(p.vC.h, p.vC.n)
        gv = p.vC.n

        @p.block.tensor
        def _(e):
            e.wait_ge(p.vC.h, gv)
            ins = None
            for mt in range(16):
                ins = e.transpose(psgt[:, mt * 128:(mt + 1) * 128],
                                  gates[:, mt * 8:(mt + 1) * 8],
                                  p.consts["id_f32"][:])
            p.tC.inc(ins)

        @p.block.vector
        def _(e):
            e.wait_ge(p.tC.h, p.tC.n)
            p.vC.inc(e.tensor_copy(out=gatesT[:], in_=psgt[:]))
            e.wait_ge(p.vC.h, p.vC.n)
        for eng in ("tensor", "scalar", "gpsimd", "sync"):
            @getattr(p.block, eng)
            def _(e):
                e.wait_ge(p.vC.h, p.vC.n)
        gstk.close()

        def pre_expert(ex, c, b=b, lnmarks=lnmarks):
            @p.block.vector
            def _(e, ex=ex, c=c):
                e.wait_ge(p.tC.h, p.tC.n)
                e.wait_ge(p.sC.h, lnmarks[min(4 * c + 3, 15)])
                ins = None
                for k in range(DG):
                    ins = e.tensor_scalar(
                        out=xeT[:, k * 512:(k + 1) * 512],
                        in0=xn2T[:, k * S + c * 512:k * S + c * 512 + 512],
                        scalar1=gcols[:, ex * 8 + k:ex * 8 + k + 1],
                        scalar2=bcols[:, ex * 8 + k:ex * 8 + k + 1],
                        op0=ALU.mult, op1=ALU.add)
                p.vC.inc(ins)
                e.wait_ge(p.vC.h, p.vC.n)

        def premov(ex, c, k, j0, w):
            return xeT[:, k * 512 + j0:k * 512 + j0 + w]

        fstk = ExitStack()
        xrb = [fstk.enter_context(p.nc.sbuf_tensor(
            f"xrb{b}{j}_{p.uid()}", [128, D], F32)) for j in range(2)]
        xov = [fstk.enter_context(p.nc.sbuf_tensor(
            f"xov{b}{j}_{p.uid()}", [128, D], F32)) for j in range(2)]
        tmpa = fstk.enter_context(p.nc.sbuf_tensor(
            f"tmpa{b}_{p.uid()}", [128, D], F32))
        xrsem = [p.r[9 + j] for j in range(2)]
        xrmark = [0, 0]
        stsem = [p.r[11 + j] for j in range(2)]
        mc, rc = lnstats[b]

        def final_fn(mtl, acc_ap, b=b, mc=mc, rc=rc):
            tile = 16 * b + mtl
            xs = mtl % 2

            @p.block.sync
            def _(e, tile=tile, xs=xs):
                e.wait_ge(p.vC.h, xrmark[xs])
                e.wait_ge(stsem[xs].h, stsem[xs].n)
                xrsem[xs].inc(e.dma_start(
                    out=xrb[xs][:],
                    in_=xsrc[tile * 128:(tile + 1) * 128, :]), 16)

            @p.block.vector
            def _(e, tile=tile, mtl=mtl, xs=xs):
                def step(ins):
                    p.vC.inc(ins)
                    e.wait_ge(p.vC.h, p.vC.n)
                e.wait_ge(xrsem[xs].h, xrsem[xs].n)
                e.wait_ge(p.vC.h, p.vC.n)
                step(e.tensor_scalar(out=tmpa[:], in0=xrb[xs][:],
                                     scalar1=mc[:, mtl:mtl + 1],
                                     scalar2=rc[:, mtl:mtl + 1],
                                     op0=ALU.subtract, op1=ALU.mult))
                step(e.scalar_tensor_tensor(out=tmpa[:], in0=tmpa[:],
                                            scalar=0.5, in1=xrb[xs][:],
                                            op0=ALU.mult, op1=ALU.add))
                xrmark[xs] = p.vC.inc(e.scalar_tensor_tensor(
                    out=xov[xs][:], in0=acc_ap, scalar=0.5, in1=tmpa[:],
                    op0=ALU.mult, op1=ALU.add))
                e.wait_ge(p.vC.h, p.vC.n)

            @p.block.sync
            def _(e, tile=tile, xs=xs):
                e.wait_ge(p.vC.h, p.vC.n)
                stsem[xs].inc(e.dma_start(
                    out=xdst[tile * 128:(tile + 1) * 128, :],
                    in_=xov[xs][:]), 16)

        emit_moe(p, xn2T, gates, gatesT,
                 lambda ex, dg, fb: w1T_scr[
                     ex, dg * 128:(dg + 1) * 128, fb * 512:(fb + 1) * 512],
                 b1cols,
                 lambda ex, ft: w2T_scr[ex, ft * 128:(ft + 1) * 128, :],
                 b2sb, FDIM, AF.Relu, final_fn, pre_expert=pre_expert,
                 premov=premov, name=f"moe{b}", tile0=0, ntiles=16)

        @p.block.sync
        def _(e):
            e.wait_ge(stsem[0].h, stsem[0].n)
            e.wait_ge(stsem[1].h, stsem[1].n)
        for eng in ("tensor", "vector", "scalar", "gpsimd"):
            @getattr(p.block, eng)
            def _(e):
                e.wait_ge(stsem[0].h, stsem[0].n)
                e.wait_ge(stsem[1].h, stsem[1].n)
        fstk.close()
        bstk.close()
    phase_end(p)





def phase_l2_attn(p, xsrc, xdst, W):
    phase_begin(p)
    NST = 16
    QW = 512
    NQ = S // QW

    wout = p.sbuf("wout", [128, DG * D], BF16)
    wxp = p.sbuf("wxp", [128, DG * 33], BF16)
    acols = p.sbuf("acols", [128, DG * NST], F32)
    wcv = p.sbuf("wcv", [128, DG * 4], F32)
    esel = p.sbuf("esel", [16, NST * 128], F32)
    e32 = p.sbuf("e32", [33, 128], F32)
    onecell = p.sbuf("onecell", [1, 1], F32)

    @p.block.gpsimd
    def _(e):
        for k in range(DG):
            p.load_serial(e, p.ldg, wout[:, k * D:(k + 1) * D],
                          W["attn_W_out"][k * 128:(k + 1) * 128, :])
        p.load_serial(e, p.ldg,
                      wxp[:].rearrange("q (k c) -> q k c", k=DG),
                      W["attn_W_xproj"].rearrange("(k q) c -> q k c", q=128))
        e.wait_ge(p.gC.h, p.gC.n)
        p.gC.inc(e.memset(esel[:], 1.0))
        p.gC.inc(e.memset(e32[:], 1.0))
        p.gC.inc(e.memset(onecell[:], 1.0))
        e.wait_ge(p.gC.h, p.gC.n)
        ins = None
        for n in range(NST):
            ins = e.affine_select(out=esel[:, n * 128:(n + 1) * 128],
                                  in_=esel[:, n * 128:(n + 1) * 128],
                                  pattern=[[0, 128]], compare_op=ALU.is_equal,
                                  fill=0.0, base=-n, channel_multiplier=1)
        p.gC.inc(ins)
        e.wait_ge(p.gC.h, p.gC.n)
        p.gC.inc(e.affine_select(out=e32[:], in_=e32[:], pattern=[[0, 128]],
                                 compare_op=ALU.is_equal, fill=0.0,
                                 base=-32, channel_multiplier=1))

    @p.block.sync
    def _(e):
        p.load_serial(e, p.ld,
                      acols[:].rearrange("q (g n) -> q g n", g=DG),
                      W["attn_log_A"].rearrange("(g q) n -> q g n", q=128))
        p.load_serial(e, p.ld,
                      wcv[:].rearrange("q (g f) -> q g f", g=DG),
                      W["attn_conv_w"].rearrange("(g q) o f -> q g (o f)",
                                                 q=128))

    @p.block.scalar
    def _(e):
        e.wait_ge(p.ld.h, p.ld.n)
        p.sC.inc(e.activation(out=acols[:], in_=acols[:], func=AF.Exp))

    @p.block.vector
    def _(e):
        e.wait_ge(p.sC.h, p.sC.n)
        p.vC.inc(e.tensor_scalar_mul(out=acols[:], in0=acols[:], scalar1=-1.0))
        e.wait_ge(p.vC.h, p.vC.n)

    cvb = load_bias_cols(p, "cvb", W["attn_conv_b"], D)
    dpc = load_bias_cols(p, "dpc", W["attn_D_param"], D)
    lnstats = [(p.sbuf(f"lm{b}", [128, 16], F32),
                p.sbuf(f"lr{b}", [128, 16], F32)) for b in range(2)]
    winb = [p.sbuf(f"winb{j}", [128, DG * 128], BF16) for j in range(2)]
    winsem = [p.r[19 + j] for j in range(2)]
    winmark = [0, 0]

    for b in range(B_PER_CORE):
        bstk = ExitStack()

        def _bs(nm, shape, dtype):
            return bstk.enter_context(p.nc.sbuf_tensor(
                f"{nm}_{p.uid()}", shape, dtype))
        xnT = _bs("xnT", [128, DG * S], BF16)
        xmc = _bs("xmc", [128, DG * 2052], BF16)
        zsil = _bs("zsil", [128, DG * S], BF16)
        x_convT = xnT
        yout = xmc

        @p.block.gpsimd
        def _(e):
            e.wait_ge(p.gC.h, p.gC.n)
            xm3 = xmc[:].rearrange("q (g w) -> q g w", g=DG)
            p.gC.inc(e.memset(xm3[:, :, 0:3], 0.0))
        pad_mark = p.gC.n

        _, mcols, rcols, lnmarks = emit_ln_dmajor(
            p, lambda i, b=b: xsrc[(16 * b + i) * 128:(16 * b + i + 1) * 128, :],
            16, S, dst=xnT, name=f"l2a_ln{b}", stats=lnstats[b])

        prB = _bs("prB", [16, S], F32)
        prC = _bs("prC", [16, S], F32)
        dlt = _bs("dlt", [1, S], F32)
        yq = _bs("yq", [128, DG * QW], F32)
        state = _bs("state", [128, DG * NST], F32)
        dbc = _bs("dbc", [128, QW], F32)
        dbq = _bs("dbq", [16, QW], F32)
        dnb = _bs("dnb", [128, QW], F32)
        cnb = _bs("cnb", [128, QW], F32)
        aq = _bs("aq", [128, QW], F32)
        buq = _bs("buq", [128, QW], F32)
        hq = _bs("hq", [128, QW], F32)
        dstk = ExitStack()
        prD = dstk.enter_context(p.nc.sbuf_tensor(
            f"prD{b}_{p.uid()}", [1, S], F32))
        srow = dstk.enter_context(p.nc.sbuf_tensor(
            f"srow{b}_{p.uid()}", [1, S], F32))

        pstk = ExitStack()
        psx = [pstk.enter_context(p.nc.psum_tensor(
            f"psx{b}{j}_{p.uid()}", [128, 512], F32)) for j in range(2)]
        psxmark = [0, 0]
        for dt in range(16):
            wsl = dt % 2

            @p.block.gpsimd
            def _(e, b=b, dt=dt, wsl=wsl):
                e.wait_ge(p.tC.h, winmark[wsl])
                for k in range(DG):
                    winsem[wsl].inc(e.dma_start(
                        out=winb[wsl][:, k * 128:(k + 1) * 128],
                        in_=W["attn_W_in"][k * 128:(k + 1) * 128,
                                           dt * 128:(dt + 1) * 128]), 16)

            for ch in range(4):
                pxp = (dt * 4 + ch) % 2

                @p.block.tensor
                def _(e, b=b, dt=dt, ch=ch, pxp=pxp, wsl=wsl):
                    if ch == 0:
                        e.wait_ge(winsem[wsl].h, winsem[wsl].n)
                        if dt == 0:
                            e.wait_ge(p.gC.h, pad_mark)
                    e.wait_ge(p.sC.h, lnmarks[15] if ch == 0 and dt == 0
                              else psxmark[pxp])
                    e.wait_ge(p.sC.h, psxmark[pxp])
                    ins = None
                    for k in range(DG):
                        ins = e.matmul(
                            psx[pxp][:],
                            winb[wsl][:, k * 128:(k + 1) * 128],
                            xnT[:, k * S + ch * 512:k * S + ch * 512 + 512],
                            start=(k == 0), stop=(k == DG - 1))
                    wm = p.tC.inc(ins)
                    if ch == 3:
                        winmark[wsl] = wm
                xm = p.tC.n

                @p.block.scalar
                def _(e, dt=dt, ch=ch, pxp=pxp, xm=xm):
                    e.wait_ge(p.tC.h, xm)
                    if dt < 8:
                        out_ap = xmc[:, dt * 2052 + 3 + ch * 512:
                                     dt * 2052 + 3 + ch * 512 + 512]
                        psxmark[pxp] = p.sC.inc(e.copy(out=out_ap,
                                                       in_=psx[pxp][:]))
                    else:
                        out_ap = zsil[:, (dt - 8) * S + ch * 512:
                                      (dt - 8) * S + ch * 512 + 512]
                        psxmark[pxp] = p.sC.inc(e.activation(
                            out=out_ap, in_=psx[pxp][:], func=AF.Silu))
        xz_done_s = p.sC.n

        cstk = ExitStack()
        tmpc = cstk.enter_context(p.nc.sbuf_tensor(
            f"tmpc{b}_{p.uid()}", [128, 512], F32))
        for dg in range(DG):
            for q in range(NQ):
                @p.block.vector
                def _(e, dg=dg, q=q):
                    def step(ins):
                        p.vC.inc(ins)
                        e.wait_ge(p.vC.h, p.vC.n)
                    if dg == 0 and q == 0:
                        e.wait_ge(p.tC.h, p.tC.n)
                    e.wait_ge(p.sC.h, xz_done_s if (dg == 0 and q == 0)
                              else p.sC.n)
                    base = dg * 2052 + 3 + q * QW
                    step(e.tensor_scalar_mul(
                        out=tmpc[:], in0=xmc[:, base - 3:base - 3 + QW],
                        scalar1=wcv[:, dg * 4 + 0:dg * 4 + 1]))
                    for tau in range(1, 4):
                        step(e.scalar_tensor_tensor(
                            out=tmpc[:], in0=xmc[:, base - 3 + tau:
                                               base - 3 + tau + QW],
                            scalar=wcv[:, dg * 4 + tau:dg * 4 + tau + 1],
                            in1=tmpc[:], op0=ALU.mult, op1=ALU.add))
                vmk = p.vC.n

                @p.block.scalar
                def _(e, dg=dg, q=q, vmk=vmk):
                    e.wait_ge(p.vC.h, vmk)
                    p.sC.inc(e.activation(
                        out=x_convT[:, dg * S + q * QW:dg * S + q * QW + QW],
                        in_=tmpc[:], func=AF.Silu,
                        bias=cvb[:, dg:dg + 1]))
        conv_done = p.sC.n
        cstk.close()

        qstk = ExitStack()
        psB = qstk.enter_context(p.nc.psum_tensor(
            f"psB{b}_{p.uid()}", [16, 512], F32))
        psC = qstk.enter_context(p.nc.psum_tensor(
            f"psC{b}_{p.uid()}", [16, 512], F32))
        psD = qstk.enter_context(p.nc.psum_tensor(
            f"psD{b}_{p.uid()}", [1, 512], F32))
        prmark = [0]
        for ch in range(4):
            @p.block.tensor
            def _(e, b=b, ch=ch):
                if ch == 0:
                    e.wait_ge(p.sC.h, conv_done)
                    e.wait_ge(p.ldg.h, p.ldg.n)
                e.wait_ge(p.vC.h, prmark[0])
                for (ps_, c0, cw) in ((psB, 0, 16), (psC, 16, 16),
                                      (psD, 32, 1)):
                    ins = None
                    for k in range(DG):
                        ins = e.matmul(
                            ps_[:],
                            wxp[:, k * 33 + c0:k * 33 + c0 + cw],
                            x_convT[:, k * S + ch * 512:k * S + ch * 512 + 512],
                            start=(k == 0), stop=(k == DG - 1))
                    p.tC.inc(ins)
            pm = p.tC.n

            @p.block.vector
            def _(e, ch=ch, pm=pm):
                def step(ins):
                    p.vC.inc(ins)
                    e.wait_ge(p.vC.h, p.vC.n)
                e.wait_ge(p.tC.h, pm)
                step(e.tensor_copy(out=prB[:, ch * 512:(ch + 1) * 512],
                                   in_=psB[:]))
                step(e.tensor_copy(out=prC[:, ch * 512:(ch + 1) * 512],
                                   in_=psC[:]))
                prmark[0] = p.vC.inc(e.tensor_copy(
                    out=prD[:, ch * 512:(ch + 1) * 512], in_=psD[:]))
                e.wait_ge(p.vC.h, p.vC.n)
        qstk.close()

        @p.block.vector
        def _(e):
            e.wait_ge(p.vC.h, p.vC.n)
            p.vC.inc(e.tensor_scalar_mul(out=srow[:], in0=prD[:],
                                         scalar1=-1.0))
            e.wait_ge(p.vC.h, p.vC.n)
            p.vC.inc(e.tensor_max(out=srow[:], in0=srow[:], in1=prD[:]))
            e.wait_ge(p.vC.h, p.vC.n)
        sp1 = p.vC.n

        @p.block.scalar
        def _(e, sp1=sp1):
            e.wait_ge(p.vC.h, sp1)
            p.sC.inc(e.activation(out=srow[:], in_=srow[:], func=AF.Exp,
                                  scale=-1.0))
            e.wait_ge(p.sC.h, p.sC.n)
            p.sC.inc(e.activation(out=srow[:], in_=srow[:], func=AF.Ln,
                                  bias=onecell[:]))
        sp2 = p.sC.n

        @p.block.vector
        def _(e, sp2=sp2):
            def step(ins):
                p.vC.inc(ins)
                e.wait_ge(p.vC.h, p.vC.n)
            e.wait_ge(p.sC.h, sp2)
            step(e.tensor_scalar_max(out=dlt[:], in0=prD[:], scalar1=0.0))
            step(e.tensor_add(out=dlt[:], in0=dlt[:], in1=srow[:]))

        for eng in ("tensor", "scalar", "gpsimd", "sync"):
            @getattr(p.block, eng)
            def _(e):
                e.wait_ge(p.vC.h, p.vC.n)
        dstk.close()

        sstk = ExitStack()
        psq1 = sstk.enter_context(p.nc.psum_tensor(
            f"psq1{b}_{p.uid()}", [128, QW], F32))
        psq2 = sstk.enter_context(p.nc.psum_tensor(
            f"psq2{b}_{p.uid()}", [128, QW], F32))
        for q in range(NQ):
            @p.block.tensor
            def _(e, q=q):
                e.wait_ge(p.vC.h, p.vC.n)
                p.tC.inc(e.matmul(psq1[:], p.consts["ones_f"][0:1, 0:128],
                                  dlt[:, q * QW:(q + 1) * QW],
                                  start=True, stop=True))
            dm = p.tC.n

            @p.block.vector
            def _(e, q=q, dm=dm):
                def step(ins):
                    p.vC.inc(ins)
                    e.wait_ge(p.vC.h, p.vC.n)
                e.wait_ge(p.tC.h, dm)
                step(e.tensor_copy(out=dbc[:], in_=psq1[:]))
                step(e.tensor_mul(out=dbq[:], in0=prB[:, q * QW:(q + 1) * QW],
                                  in1=dbc[0:16, :]))
                if q == 0:
                    step(e.memset(yq[:], 0.0))
                else:
                    step(e.memset(yq[:], 0.0))
            dbm = p.vC.n

            for n in range(NST):
                @p.block.tensor
                def _(e, q=q, n=n, dbm=dbm):
                    e.wait_ge(p.vC.h, p.vC.n)
                    p.tC.inc(e.matmul(psq1[:], esel[:, n * 128:(n + 1) * 128],
                                      dbq[:], start=True, stop=True))
                    p.tC.inc(e.matmul(psq2[:], esel[:, n * 128:(n + 1) * 128],
                                      prC[:, q * QW:(q + 1) * QW],
                                      start=True, stop=True))
                sm = p.tC.n

                @p.block.vector
                def _(e, sm=sm):
                    def step(ins):
                        p.vC.inc(ins)
                        e.wait_ge(p.vC.h, p.vC.n)
                    e.wait_ge(p.tC.h, sm)
                    step(e.tensor_copy(out=dnb[:], in_=psq1[:]))
                    step(e.tensor_copy(out=cnb[:], in_=psq2[:]))
                dn = p.vC.n

                for dg in range(DG):
                    @p.block.scalar
                    def _(e, q=q, n=n, dg=dg, dn=dn):
                        e.wait_ge(p.vC.h, p.vC.n)
                        e.wait_ge(p.sC.h, p.sC.n)
                        p.sC.inc(e.activation(
                            out=aq[:], in_=dbc[:], func=AF.Exp,
                            scale=acols[:, dg * NST + n:dg * NST + n + 1]))
                    am = p.sC.n

                    @p.block.vector
                    def _(e, q=q, n=n, dg=dg, am=am):
                        def step(ins):
                            p.vC.inc(ins)
                            e.wait_ge(p.vC.h, p.vC.n)
                        e.wait_ge(p.sC.h, am)
                        step(e.tensor_mul(
                            out=buq[:], in0=dnb[:],
                            in1=x_convT[:, dg * S + q * QW:
                                        dg * S + q * QW + QW]))
                        sc = state[:, dg * NST + n:dg * NST + n + 1]
                        ini = 0.0 if q == 0 else sc
                        step(e.tensor_tensor_scan(
                            out=hq[:], data0=aq[:], data1=buq[:],
                            initial=ini, op0=ALU.mult, op1=ALU.add))
                        step(e.tensor_copy(out=sc, in_=hq[:, QW - 1:QW]))
                        step(e.tensor_mul(out=buq[:], in0=cnb[:], in1=hq[:]))
                        step(e.tensor_add(
                            out=yq[:, dg * QW:(dg + 1) * QW],
                            in0=yq[:, dg * QW:(dg + 1) * QW], in1=buq[:]))

            for dg in range(DG):
                @p.block.vector
                def _(e, q=q, dg=dg):
                    def step(ins):
                        p.vC.inc(ins)
                        e.wait_ge(p.vC.h, p.vC.n)
                    uap = x_convT[:, dg * S + q * QW:dg * S + q * QW + QW]
                    zap = zsil[:, dg * S + q * QW:dg * S + q * QW + QW]
                    yap = yq[:, dg * QW:(dg + 1) * QW]
                    step(e.scalar_tensor_tensor(
                        out=yap, in0=uap, scalar=dpc[:, dg:dg + 1], in1=yap,
                        op0=ALU.mult, op1=ALU.add))
                    step(e.tensor_mul(
                        out=yout[:, dg * 2052 + q * QW:dg * 2052 + q * QW + QW],
                        in0=yap, in1=zap))
        sstk.close()
        scan_done = p.vC.n

        wstk = ExitStack()
        xrb = [wstk.enter_context(p.nc.sbuf_tensor(
            f"xrb{b}{j}_{p.uid()}", [128, D], F32)) for j in range(2)]
        xov = [wstk.enter_context(p.nc.sbuf_tensor(
            f"xov{b}{j}_{p.uid()}", [128, D], F32)) for j in range(2)]
        psres = [wstk.enter_context(p.nc.psum_tensor(
            f"psres{b}{j}_{p.uid()}", [128, D], F32)) for j in range(2)]
        xrsem = [p.r[9 + j] for j in range(2)]
        xrmark = [0, 0]
        stsem = [p.r[11 + j] for j in range(2)]
        resmark = [0, 0]
        for mt in range(16):
            tile = 16 * b + mt
            pr = mt % 2
            xs = mt % 2

            @p.block.sync
            def _(e, tile=tile, xs=xs):
                e.wait_ge(p.vC.h, xrmark[xs])
                e.wait_ge(stsem[xs].h, stsem[xs].n)
                xrsem[xs].inc(e.dma_start(
                    out=xrb[xs][:],
                    in_=xsrc[tile * 128:(tile + 1) * 128, :]), 16)

            @p.block.tensor
            def _(e, b=b, mt=mt, pr=pr, scan_done=scan_done):
                if mt == 0:
                    e.wait_ge(p.vC.h, scan_done)
                e.wait_ge(p.vC.h, resmark[pr])
                ins = None
                for dg in range(DG):
                    for n0 in (0, 512):
                        ins = e.matmul(
                            psres[pr][:, n0:n0 + 512],
                            yout[:, dg * 2052 + mt * 128:dg * 2052 + mt * 128 + 128],
                            wout[:, dg * D + n0:dg * D + n0 + 512],
                            start=(dg == 0), stop=(dg == DG - 1))
                p.tC.inc(ins)
            wm = p.tC.n

            @p.block.vector
            def _(e, tile=tile, xs=xs, pr=pr, wm=wm):
                e.wait_ge(p.tC.h, wm)
                e.wait_ge(xrsem[xs].h, xrsem[xs].n)
                resmark[pr] = p.vC.inc(e.tensor_add(
                    out=xov[xs][:], in0=psres[pr][:], in1=xrb[xs][:]))
                e.wait_ge(p.vC.h, p.vC.n)
                xrmark[xs] = p.vC.n

            @p.block.sync
            def _(e, tile=tile, xs=xs):
                e.wait_ge(p.vC.h, p.vC.n)
                stsem[xs].inc(e.dma_start(
                    out=xdst[tile * 128:(tile + 1) * 128, :],
                    in_=xov[xs][:]), 16)

        @p.block.sync
        def _(e):
            e.wait_ge(stsem[0].h, stsem[0].n)
            e.wait_ge(stsem[1].h, stsem[1].n)
        for eng in ("tensor", "vector", "scalar", "gpsimd"):
            @getattr(p.block, eng)
            def _(e):
                e.wait_ge(stsem[0].h, stsem[0].n)
                e.wait_ge(stsem[1].h, stsem[1].n)
        wstk.close()
        pstk.close()
        bstk.close()
    phase_end(p)



INPUT_SPECS = [
    ("L0_attn_x", (16, 2048, 1024)), ("L0_attn_Wq", (1024, 1024)),
    ("L0_attn_Wk", (1024, 1024)), ("L0_attn_Wv", (1024, 1024)),
    ("L0_attn_Wo", (1024, 1024)), ("L0_attn_quantum_phase", (16, 64)),
    ("L0_attn_quantum_amplitude", (16, 64)),
    ("L0_ffn_q_w", (1024, 1024)), ("L0_ffn_q_b", (1024,)),
    ("L0_ffn_k_w", (1024, 512)), ("L0_ffn_k_b", (512,)),
    ("L0_ffn_v_w", (1024, 512)), ("L0_ffn_v_b", (512,)),
    ("L0_ffn_o_w", (1024, 1024)), ("L0_ffn_o_b", (1024,)),
    ("L0_ffn_router_w", (1024, 8)), ("L0_ffn_router_b", (8,)),
    ("L0_ffn_w1", (8, 1024, 2048)), ("L0_ffn_b1", (8, 2048)),
    ("L0_ffn_w2", (8, 2048, 1024)), ("L0_ffn_b2", (8, 1024)),
    ("L1_attn_W_qkv", (1024, 3072)), ("L1_attn_b_qkv", (3072,)),
    ("L1_attn_W_out", (1024, 1024)), ("L1_attn_b_out", (1024,)),
    ("L1_attn_rel_bias", (2049, 16)),
    ("L1_ffn_w1", (1024, 4096)), ("L1_ffn_b1", (4096,)),
    ("L1_ffn_w2", (4096, 1024)), ("L1_ffn_b2", (1024,)),
    ("L2_attn_W_in", (1024, 2048)), ("L2_attn_conv_w", (1024, 1, 4)),
    ("L2_attn_conv_b", (1024,)), ("L2_attn_W_xproj", (1024, 33)),
    ("L2_attn_log_A", (1024, 16)), ("L2_attn_D_param", (1024,)),
    ("L2_attn_W_out", (1024, 1024)),
    ("L2_ffn_centroids", (8, 1024)), ("L2_ffn_ln_g", (8, 1024)),
    ("L2_ffn_ln_b", (8, 1024)), ("L2_ffn_w1", (8, 4096, 1024)),
    ("L2_ffn_b1", (8, 4096)), ("L2_ffn_w2", (8, 1024, 4096)),
    ("L2_ffn_b2", (8, 1024)),
    ("L3_attn_W_qkv", (1024, 3072)), ("L3_attn_b_qkv", (3072,)),
    ("L3_attn_W_out", (1024, 1024)), ("L3_attn_b_out", (1024,)),
    ("L3_ffn_ln1_g", (1024,)), ("L3_ffn_ln1_b", (1024,)),
    ("L3_ffn_ln2_g", (1024,)), ("L3_ffn_ln2_b", (1024,)),
    ("L3_ffn_Wq", (1024, 1024)), ("L3_ffn_Wk", (1024, 1024)),
    ("L3_ffn_Wv", (1024, 1024)), ("L3_ffn_Wo", (1024, 1024)),
    ("L3_ffn_Wg", (1024, 8)), ("L3_ffn_W1", (8, 1024, 4096)),
    ("L3_ffn_b1", (8, 4096)), ("L3_ffn_W2", (8, 4096, 1024)),
    ("L3_ffn_b2", (8, 1024)),
]

PHASES = [
    ("l0_ffn", "L0_ffn"), ("l0_attn", "L0_attn"), ("l0_ffn", "L0_ffn"),
    ("l1_ffn", "L1_ffn"), ("l1_attn", "L1_attn"), ("l1_ffn", "L1_ffn"),
    ("l2_ffn", "L2_ffn"), ("l2_attn", "L2_attn"), ("l2_ffn", "L2_ffn"),
    ("l3_ffn", "L3_ffn"), ("l3_attn", "L3_attn"), ("l3_ffn", "L3_ffn"),
]


class LazyAps(dict):
    def __init__(self, nc):
        super().__init__()
        self.nc = nc
        self.shapes = dict(INPUT_SPECS)

    def __missing__(self, name):
        if name == "L0_attn_x":
            ap = self.nc.dram_tensor("xin", [T, D], F32,
                                     kind="ExternalInput").ap()
        else:
            ap = self.nc.dram_tensor(name, list(self.shapes[name]), F32,
                                     kind="ExternalInput").ap()
        self[name] = ap
        return ap


def build_program(phase_filter=None):
    nc = bass.Bass("TRN2", target_bir_lowering=False)
    aps = LazyAps(nc)
    xout = nc.dram_tensor("xout", [T, D], F32, kind="ExternalOutput").ap()
    xsA = nc.dram_tensor("xscrA", [T, D], F32, kind="Internal").ap()
    xsB = nc.dram_tensor("xscrB", [T, D], F32, kind="Internal").ap()
    scratch = {"A": xsA, "B": xsB}

    phases = [(i, fn, pref) for i, (fn, pref) in enumerate(PHASES)
              if phase_filter is None or i in phase_filter]

    with ExitStack() as stack:
        block = stack.enter_context(nc.Block())
        p = P(nc, block, stack)
        emit_consts(p)
        cur = aps["L0_attn_x"]
        pingpong = ["A", "B"]

        class WView:
            def __init__(self, pref):
                self.pref = pref

            def __getitem__(self, k):
                return aps[self.pref.split("_")[0] + "_" + k]

        occs = {}
        for j, (i, fn, pref) in enumerate(phases):
            dst = xout if j == len(phases) - 1 else scratch[pingpong[j % 2]]
            W = WView(pref)
            occ = occs.get(fn, 0)
            occs[fn] = occ + 1
            if fn in ("l2_ffn", "l0_ffn", "l3_ffn"):
                globals()["phase_" + fn](p, cur, dst, W, occ=occ)
            else:
                globals()["phase_" + fn](p, cur, dst, W)
            cur = dst
    return nc, sorted(aps.keys())


FUSED = True


def kernel(**inputs):
    x = np.ascontiguousarray(np.asarray(inputs["L0_attn_x"], dtype=np.float32))
    cur = [np.ascontiguousarray(
        x[c * B_PER_CORE:(c + 1) * B_PER_CORE].reshape(T, D))
        for c in range(NCORES)]
    groups = ([set(range(len(PHASES)))] if FUSED
              else [{i} for i in range(len(PHASES))])
    for g in groups:
        nc, used = build_program(phase_filter=g)
        in_maps = []
        for core in range(NCORES):
            m = {}
            for name in used:
                if name == "L0_attn_x":
                    m["xin"] = cur[core]
                else:
                    m[name] = np.ascontiguousarray(
                        np.asarray(inputs[name], dtype=np.float32))
            in_maps.append(m)
        res = run_bass_kernel_spmd(nc, in_maps, core_ids=list(range(NCORES)))
        cur = [np.asarray(r["xout"]) for r in res.results]
    out = np.concatenate(
        [c.reshape(B_PER_CORE, S, D) for c in cur], axis=0)
    return out.astype(np.float32)



def emit_bias_row_to_cols64(p, cols_tensor, h, hd):
    f0 = h * hd
    return cols_tensor[f0 % 128:(f0 % 128) + hd, f0 // 128:f0 // 128 + 1]


def emit_mha_batch(p, b, cfg):
    HD, NH, S_, CHQ = cfg["HD"], cfg["NH"], cfg["S"], cfg["CHQ"]
    VD = cfg.get("VD", HD)
    NKT = S_ // 128
    NCQ = S_ // CHQ
    qT, kT = cfg["qT"], cfg["kT"]
    qsq, cc, negc = cfg["qsq"], cfg["cc"], cfg["negc"]
    psA, psO, psR = cfg["psA"], cfg["psO"], cfg["psR"]
    pexp = cfg["pexp"]
    tmps = cfg["tmps"]
    recb = cfg["recb"]
    marks = {"psA": [0, 0], "psO": [0, 0], "pexp": [0, 0], "tmps": [0, 0]}

    for h in range(NH):
        cfg["proj_qk"](h, qT, kT)
        if cfg.get("bias_ready"):
            cfg["bias_ready"](h)

        for which, src in ((0, qT), (1, kT)):
            @p.block.scalar
            def _(e, src=src):
                e.wait_ge(p.tC.h, p.tC.n)
                e.wait_ge(p.vC.h, p.vC.n)
                p.sC.inc(e.activation(out=qsq[0:HD, :], in_=src[0:HD, :],
                                      func=AF.Square))
            sqm = p.sC.n
            for jj, j0 in enumerate(range(0, S_, 512)):
                @p.block.tensor
                def _(e, j0=j0, jj=jj, sqm=sqm):
                    if jj == 0:
                        e.wait_ge(p.sC.h, sqm)
                    e.wait_ge(p.vC.h, p.vC.n)
                    p.tC.inc(e.matmul(psR[0:1, 0:512],
                                      p.consts["ones_colb"][0:HD, :],
                                      qsq[0:HD, j0:j0 + 512],
                                      start=True, stop=True))
                mmm = p.tC.n

                @p.block.vector
                def _(e, which=which, jj=jj, mmm=mmm):
                    e.wait_ge(p.tC.h, mmm)
                    p.vC.inc(e.reduce_max(out=cc[:, 4 + jj:5 + jj],
                                          in_=psR[0:1, 0:512], axis=AX.X))
                    e.wait_ge(p.vC.h, p.vC.n)

            @p.block.vector
            def _(e, which=which):
                p.vC.inc(e.reduce_max(out=cc[:, which:which + 1],
                                      in_=cc[:, 4:4 + S_ // 512], axis=AX.X))
                e.wait_ge(p.vC.h, p.vC.n)

        @p.block.vector
        def _(e):
            p.vC.inc(e.tensor_mul(out=cc[:, 2:3], in0=cc[:, 0:1],
                                  in1=cc[:, 1:2]))
            e.wait_ge(p.vC.h, p.vC.n)

        @p.block.scalar
        def _(e):
            e.wait_ge(p.vC.h, p.vC.n)
            p.sC.inc(e.sqrt(out=cc[:, 3:4], in_=cc[:, 2:3]))

        @p.block.vector
        def _(e):
            e.wait_ge(p.sC.h, p.sC.n)
            if cfg.get("extra_c") is not None:
                ins = e.tensor_scalar(out=cc[:, 3:4], in0=cc[:, 3:4],
                                      scalar1=-float(cfg["scale"]),
                                      scalar2=cfg["extra_c"],
                                      op0=ALU.mult, op1=ALU.subtract)
            else:
                ins = e.tensor_scalar_mul(out=cc[:, 3:4], in0=cc[:, 3:4],
                                          scalar1=-float(cfg["scale"]))
            p.vC.inc(ins)
            e.wait_ge(p.vC.h, p.vC.n)
        ncm = p.vC.n

        @p.block.tensor
        def _(e):
            e.wait_ge(p.vC.h, ncm)
            p.tC.inc(e.matmul(psR[0:128, 0:1], p.consts["ones_f"][0:1, 0:128],
                              cc[:, 3:4], start=True, stop=True))
        nbm = p.tC.n

        @p.block.vector
        def _(e):
            e.wait_ge(p.tC.h, nbm)
            p.vC.inc(e.tensor_copy(out=negc[:], in_=psR[0:128, 0:1]))
        negc_mark = p.vC.n

        for ci in range(NCQ):
            c0 = ci * CHQ
            po = ci % 2
            kts = [kt for kt in range(NKT) if cfg["mask"](kt, c0) != "skip"]
            nk = len(kts)
            for gi in range((nk + 1) // 2):
                pair = kts[gi * 2:gi * 2 + 2]
                pa = gi % 2
                width = len(pair) * CHQ
                for jj, kt in enumerate(pair):
                    @p.block.tensor
                    def _(e, kt=kt, jj=jj, gi=gi, pa=pa, po=po, c0=c0):
                        if gi == 0 and jj == 0:
                            e.wait_ge(p.vC.h, negc_mark)
                            e.wait_ge(p.vC.h, marks["psO"][po])
                        if jj == 0:
                            if cfg.get("bias_slice_any"):
                                e.wait_ge(p.vC.h, marks["psA"][pa])
                            else:
                                e.wait_ge(p.sC.h, marks["psA"][pa])
                        p.tC.inc(e.matmul(
                            psA[pa][:, jj * CHQ:(jj + 1) * CHQ],
                            kT[0:HD, kt * 128:(kt + 1) * 128],
                            qT[0:HD, c0:c0 + CHQ],
                            start=True, stop=True))
                smark = p.tC.n

                bias_aps = [cfg["bias_slice"](h, kt, c0) for kt in pair]
                if bias_aps[0] is not None:
                    @p.block.vector
                    def _(e, h=h, pa=pa, bias_aps=bias_aps, smark=smark):
                        e.wait_ge(p.tC.h, smark)
                        e.wait_ge(p.sC.h, marks["tmps"][pa])
                        if cfg.get("bias_wait"):
                            cfg["bias_wait"](e, h)
                        ins = None
                        for jj, bias_ap in enumerate(bias_aps):
                            ins = e.scalar_tensor_tensor(
                                out=tmps[pa][:, jj * CHQ:(jj + 1) * CHQ],
                                in0=psA[pa][:, jj * CHQ:(jj + 1) * CHQ],
                                scalar=float(cfg["scale"]), in1=bias_ap,
                                op0=ALU.mult, op1=ALU.add)
                        marks["psA"][pa] = p.vC.inc(ins)
                    vmark = p.vC.n

                    @p.block.scalar
                    def _(e, pa=pa, width=width, vmark=vmark):
                        e.wait_ge(p.vC.h, vmark)
                        e.wait_ge(p.tC.h, marks["pexp"][pa])
                        marks["tmps"][pa] = p.sC.inc(e.activation(
                            out=pexp[pa][:, 0:width],
                            in_=tmps[pa][:, 0:width], func=AF.Exp,
                            bias=negc[:], scale=1.0))
                else:
                    @p.block.scalar
                    def _(e, pa=pa, width=width, smark=smark):
                        e.wait_ge(p.tC.h, smark)
                        e.wait_ge(p.tC.h, marks["pexp"][pa])
                        marks["psA"][pa] = p.sC.inc(e.activation(
                            out=pexp[pa][:, 0:width],
                            in_=psA[pa][:, 0:width], func=AF.Exp,
                            bias=negc[:], scale=float(cfg["scale"])))
                emark = p.sC.n
                mmark = 0

                ms = [cfg["mask"](kt, c0) for kt in pair]
                if any(m is not None for m in ms):
                    @p.block.vector
                    def _(e, pa=pa, ms=ms, emark=emark):
                        e.wait_ge(p.sC.h, emark)
                        ins = None
                        for jj, m in enumerate(ms):
                            if m is None:
                                continue
                            ins = e.tensor_mul(
                                out=pexp[pa][:, jj * CHQ:(jj + 1) * CHQ],
                                in0=pexp[pa][:, jj * CHQ:(jj + 1) * CHQ],
                                in1=m)
                        p.vC.inc(ins)
                        e.wait_ge(p.vC.h, p.vC.n)
                    mmark = p.vC.n

                @p.block.tensor
                def _(e, h=h, pair=pair, gi=gi, pa=pa, po=po, nk=nk,
                      ms=ms, mmark=mmark, emark=emark):
                    e.wait_ge(p.sC.h, emark)
                    if mmark:
                        e.wait_ge(p.vC.h, mmark)
                    ins = None
                    for jj, kt in enumerate(pair):
                        ki = gi * 2 + jj
                        ins = e.matmul(psO[po][:], cfg["vslice"](kt, h),
                                       pexp[pa][:, jj * CHQ:(jj + 1) * CHQ],
                                       start=(ki == 0), stop=(ki == nk - 1))
                    marks["pexp"][pa] = p.tC.inc(ins)
            pvmark = p.tC.n

            @p.block.vector
            def _(e, po=po):
                e.wait_ge(p.tC.h, pvmark)
                e.wait_ge(p.vC.h, p.vC.n)
                p.vC.inc(e.reciprocal(out=recb[0:1, 0:CHQ],
                                      in_=psO[po][VD:VD + 1, 0:CHQ]))
                e.wait_ge(p.vC.h, p.vC.n)
            rmark = p.vC.n

            @p.block.tensor
            def _(e):
                e.wait_ge(p.vC.h, rmark)
                p.tC.inc(e.matmul(psR[0:VD, 0:CHQ],
                                  p.consts["ones_f"][0:1, 0:VD],
                                  recb[0:1, 0:CHQ], start=True, stop=True))
            bmark = p.tC.n

            @p.block.vector
            def _(e, h=h, c0=c0, po=po):
                e.wait_ge(p.tC.h, bmark)
                p.vC.inc(e.tensor_copy(out=recb[0:VD, 0:CHQ],
                                       in_=psR[0:VD, 0:CHQ]))
                e.wait_ge(p.vC.h, p.vC.n)
                marks["psO"][po] = p.vC.inc(e.tensor_mul(
                    out=cfg["attn_slice"](h, c0),
                    in0=psO[po][0:VD, 0:CHQ], in1=recb[0:VD, 0:CHQ]))



def phase_l1_attn(p, xsrc, xdst, W):
    return phase_qkv_attn(p, xsrc, xdst, W, relbias=True)


def phase_l3_attn(p, xsrc, xdst, W):
    return phase_qkv_attn(p, xsrc, xdst, W, causal=True)


def phase_qkv_attn(p, xsrc, xdst, W, relbias=False, causal=False):
    phase_begin(p)
    HD, NH = 64, 16
    CHQ = 512

    wv = p.sbuf("wv", [128, DG * 1024], BF16)
    wo = p.sbuf("wo", [128, DG * 1024], BF16)

    @p.block.gpsimd
    def _(e):
        for k in range(DG):
            p.load_serial(e, p.ldg, wv[:, k * 1024:(k + 1) * 1024],
                          W["attn_W_qkv"][k * 128:(k + 1) * 128, 2048:3072])
        for k in range(DG):
            p.load_serial(e, p.ldg, wo[:, k * 1024:(k + 1) * 1024],
                          W["attn_W_out"][k * 128:(k + 1) * 128, :])

    bq_cols = load_bias_cols(p, "bqkv", W["attn_b_qkv"], 3072)
    bo_b = load_bcast(p, "bob", W["attn_b_out"], D)
    vb_b2 = load_bcast(p, "vbb", W["attn_b_qkv"][2048:3072], D)

    if causal:
        cmasks = {}
        cm = p.sbuf("cmasks", [128, 4 * CHQ], BF16)

        @p.block.gpsimd
        def _(e):
            p.gC.inc(e.memset(cm[:], 1.0))
            e.wait_ge(p.gC.h, p.gC.n)
            ins = None
            for oi, off in enumerate((0, 128, 256, 384)):
                ins = e.affine_select(
                    out=cm[:, oi * CHQ:(oi + 1) * CHQ],
                    in_=cm[:, oi * CHQ:(oi + 1) * CHQ],
                    pattern=[[1, CHQ]], compare_op=ALU.is_ge, fill=0.0,
                    base=-off, channel_multiplier=-1)
            p.gC.inc(ins)
        cmask_gmark = p.gC.n
        for oi, off in enumerate((0, 128, 256, 384)):
            cmasks[off] = cm[:, oi * CHQ:(oi + 1) * CHQ]

        def mask_fn(kt, c0):
            if kt * 128 > c0 + CHQ - 1:
                return "skip"
            off = kt * 128 - c0
            if off in cmasks:
                return cmasks[off]
            return None
    else:
        def mask_fn(kt, c0):
            return None

    if relbias:
        rb = W["attn_rel_bias"]
        relmax = p.sbuf("relmax", [16, 4], F32)
        prep = ExitStack()

        def _ps(name, shape, dtype):
            return prep.enter_context(
                p.nc.sbuf_tensor(f"{name}_{p.uid()}", shape, dtype))
        rb2048 = _ps("rb2048", [128, 16 * 16], F32)
        rblast = _ps("rblast", [1, 16], F32)
        relT = _ps("relT", [16, 2064], F32)
        bpad = _ps("bpad", [16, 4096], BF16)

        @p.block.sync
        def _(e):
            p.load_serial(e, p.ld,
                          rb2048[:].rearrange("q (t h) -> q t h", t=16),
                          rb[0:2048, :].rearrange("(t q) h -> q t h", q=128))
            p.load_serial(e, p.ld, rblast[:], rb[2048:2049, :])

        ptt = prep.enter_context(p.nc.psum_tensor("ptt", [16, 2064], F32))

        @p.block.tensor
        def _(e):
            e.wait_ge(p.ld.h, p.ld.n)
            for t in range(16):
                e.transpose(ptt[:, t * 128:(t + 1) * 128],
                            rb2048[:, t * 16:(t + 1) * 16],
                            p.consts["id_f32"][:])
            p.tC.inc(e.transpose(ptt[:, 2048:2049], rblast[:],
                                 p.consts["id_f32"][0:1, 0:1]))

        @p.block.vector
        def _(e):
            def step(ins):
                p.vC.inc(ins)
                e.wait_ge(p.vC.h, p.vC.n)
            e.wait_ge(p.tC.h, p.tC.n)
            step(e.tensor_copy(out=relT[:, 0:2049], in_=ptt[:, 0:2049]))
            step(e.reduce_max(out=relmax[:, 0:1], in_=relT[:, 0:2049],
                              axis=AX.X, apply_absolute_value=True))

        ptm = prep.enter_context(p.nc.psum_tensor("ptm", [1, 16], F32))

        @p.block.tensor
        def _(e):
            e.wait_ge(p.vC.h, p.vC.n)
            p.tC.inc(e.transpose(ptm[:], relmax[:, 0:1],
                                 p.consts["id_f32"][0:16, 0:16]))

        @p.block.vector
        def _(e):
            e.wait_ge(p.tC.h, p.tC.n)
            p.vC.inc(e.reduce_max(out=relmax[0:1, 2:3], in_=ptm[:], axis=AX.X))
            e.wait_ge(p.vC.h, p.vC.n)

        @p.block.gpsimd
        def _(e):
            e.wait_ge(p.vC.h, p.vC.n)
            p.gC.inc(e.memset(bpad[:], 1.0))
            e.wait_ge(p.gC.h, p.gC.n)

        @p.block.vector
        def _(e):
            def step(ins):
                p.vC.inc(ins)
                e.wait_ge(p.vC.h, p.vC.n)
            e.wait_ge(p.gC.h, p.gC.n)
            step(e.tensor_single_scalar(out=bpad[:, 0:1023], in_=bpad[:, 0:1023],
                                        scalar=relT[:, 0:1], op=ALU.mult))
            step(e.tensor_single_scalar(out=bpad[:, 3072:4096],
                                        in_=bpad[:, 3072:4096],
                                        scalar=relT[:, 2048:2049], op=ALU.mult))
            step(e.tensor_copy(out=bpad[:, 1023:3072], in_=relT[:, 0:2049]))
        bpad_mark = p.vC.n

        bpad_hbm = p.nc.dram_tensor(f"bpadscr_{p.uid()}", [16, 4096], BF16,
                                    kind="Internal").ap()

        @p.block.sync
        def _(e):
            e.wait_ge(p.vC.h, bpad_mark)
            p.load_serial(e, p.ld, bpad_hbm, bpad[:])

        prep.close()
        for eng in ("tensor", "vector", "scalar", "gpsimd"):
            @getattr(p.block, eng)
            def _(e):
                e.wait_ge(p.ld.h, p.ld.n)

        strip = [p.sbuf(f"strip{j}", [128, 3968], BF16) for j in range(2)]
        ssem = [p.r[13 + j] for j in range(2)]
        smarks = {}

        def bias_ready(h):
            sl = h % 2

            @p.block.sync
            def _(e, h=h, sl=sl):
                e.wait_ge(p.ld.h, p.ld.n)
                e.wait_ge(p.vC.h, p.vC.n)
                for q in range(128):
                    ssem[sl].inc(e.dma_start(
                        out=strip[sl][q:q + 1, :],
                        in_=bpad_hbm[h:h + 1, 127 - q:127 - q + 3968]), 16)
            smarks[h] = ssem[sl].n

        def bias_wait(e, h):
            e.wait_ge(ssem[h % 2].h, smarks[h])

        def bias_slice(h, kt, c0):
            sl = h % 2
            return strip[sl][:, c0 - kt * 128 + 1920:
                             c0 - kt * 128 + 1920 + CHQ]
        extra_c_ap = relmax[0:1, 2:3]
    else:
        bias_ready = None
        bias_wait = None
        extra_c_ap = None

        def bias_slice(h, kt, c0):
            return None

    xnT = p.sbuf("xnTb", [128, DG * S], BF16)
    vaug = p.sbuf("vaug", [128, 16 * NH * (HD + 1)], BF16)
    attnT = p.sbuf("attnT", [128, DG * S], BF16)
    qkw = [p.sbuf(f"qkw{j}", [128, DG * 128], BF16) for j in range(2)]
    qksem = [p.r[15 + j] for j in range(2)]
    qkmark = [0, 0]

    def qk_stream(h):
        sl = h % 2

        @p.block.gpsimd
        def _(e, h=h, sl=sl):
            if h >= 2 or True:
                e.wait_ge(p.tC.h, qkmark[sl])
            for k in range(DG):
                qksem[sl].inc(e.dma_start(
                    out=qkw[sl][:, k * 128:k * 128 + 64],
                    in_=W["attn_W_qkv"][k * 128:(k + 1) * 128,
                                        h * HD:h * HD + HD]), 16)
                qksem[sl].inc(e.dma_start(
                    out=qkw[sl][:, k * 128 + 64:k * 128 + 128],
                    in_=W["attn_W_qkv"][k * 128:(k + 1) * 128,
                                        1024 + h * HD:1024 + h * HD + HD]), 16)

    @p.block.gpsimd
    def _(e):
        va = vaug[:].rearrange("q (t h c) -> q t h c", t=16, h=NH)
        p.gC.inc(e.memset(va[:, :, :, HD:HD + 1], 1.0))
    ones_mark = p.gC.n

    for b in range(B_PER_CORE):
        xnT_, mcols, rcols, lnmarks = emit_ln_dmajor(
            p, lambda i, b=b: xsrc[(16 * b + i) * 128:(16 * b + i + 1) * 128, :],
            16, S, dst=xnT, name=f"lnb{b}")

        def xnT_v(k, j0, w):
            return xnT[:, k * S + j0:k * S + j0 + w]

        vstk = ExitStack()
        psv = [vstk.enter_context(p.nc.psum_tensor(
            f"psv{b}{j}_{p.uid()}", [128, D], F32)) for j in range(2)]
        psvmark = [0, 0]
        for mt in range(16):
            pv = mt % 2

            @p.block.tensor
            def _(e, b=b, mt=mt, pv=pv):
                if mt == 0:
                    e.wait_ge(p.gC.h, p.gC.n)
                    e.wait_ge(p.ldg.h, p.ldg.n)
                e.wait_ge(p.sC.h, lnmarks[mt])
                e.wait_ge(p.vC.h, psvmark[pv])
                ins = None
                for k in range(DG):
                    for n0 in (0, 512):
                        ins = e.matmul(
                            psv[pv][:, n0:n0 + 512],
                            xnT[:, k * S + mt * 128:k * S + mt * 128 + 128],
                            wv[:, k * 1024 + n0:k * 1024 + n0 + 512],
                            start=(k == 0), stop=(k == DG - 1))
                p.tC.inc(ins)
            vm = p.tC.n

            @p.block.vector
            def _(e, b=b, mt=mt, pv=pv, vm=vm):
                e.wait_ge(p.tC.h, vm)
                ov = vaug[:, mt * NH * (HD + 1):(mt + 1) * NH * (HD + 1)]
                ov = ov.rearrange("q (h c) -> q h c", h=NH)[:, :, 0:HD]
                iv = psv[pv][:].rearrange("q (h d) -> q h d", h=NH)
                bv = vb_b2[:].rearrange("q (h d) -> q h d", h=NH)
                psvmark[pv] = p.vC.inc(e.tensor_add(out=ov, in0=iv, in1=bv))

        @p.block.tensor
        def _(e):
            e.wait_ge(p.vC.h, p.vC.n)
        vstk.close()

        hstk = ExitStack()

        def _hs(nm, shape, dtype):
            return hstk.enter_context(
                p.nc.sbuf_tensor(f"{nm}_{p.uid()}", shape, dtype))
        qT = _hs("qTh", [HD, S], BF16)
        kT = _hs("kTh", [HD, S], BF16)
        qsq = _hs("qsq", [HD, S], BF16)
        cc = _hs("cc", [1, 8], F32)
        negc = _hs("negc", [128, 1], F32)
        pexp = [_hs(f"pexp{j}", [128, 2 * CHQ], BF16) for j in range(2)]
        tmps = [_hs(f"tmps{j}", [128, 2 * CHQ], F32) for j in range(2)]
        recb = _hs("recb", [HD, CHQ], F32)
        psA = [hstk.enter_context(p.nc.psum_tensor(
            f"psA{b}{j}_{p.uid()}", [128, 2 * CHQ], F32)) for j in range(2)]
        psO = [hstk.enter_context(p.nc.psum_tensor(
            f"psO{b}{j}_{p.uid()}", [HD + 1, CHQ], F32)) for j in range(2)]
        psR = hstk.enter_context(p.nc.psum_tensor(
            f"psR{b}_{p.uid()}", [128, CHQ], F32))
        psq = hstk.enter_context(p.nc.psum_tensor(
            f"psq{b}_{p.uid()}", [HD, CHQ], F32))
        pqmark = [0]

        def proj_qk(h, qT_, kT_, b=b):
            qk_stream(h)
            sl = h % 2
            qkn = qksem[sl].n
            for which in (0, 1):
                dstT = qT_ if which == 0 else kT_
                bcol = emit_bias_row_to_cols64(
                    p, bq_cols, (which * 1024 + h * HD) // HD, HD)
                for ci4 in range(S // 512):
                    @p.block.tensor
                    def _(e, h=h, sl=sl, which=which, ci4=ci4, qkn=qkn):
                        e.wait_ge(qksem[sl].h, qkn)
                        e.wait_ge(p.sC.h, pqmark[0])
                        ins = None
                        for k in range(DG):
                            ins = e.matmul(
                                psq[:, 0:512],
                                qkw[sl][:, k * 128 + which * 64:
                                        k * 128 + which * 64 + 64],
                                xnT_v(k, ci4 * 512, 512),
                                start=(k == 0), stop=(k == DG - 1))
                        qkmark[sl] = p.tC.inc(ins)
                    pm = p.tC.n

                    @p.block.scalar
                    def _(e, dstT=dstT, bcol=bcol, ci4=ci4, pm=pm):
                        e.wait_ge(p.tC.h, pm)
                        pqmark[0] = p.sC.inc(e.add(
                            out=dstT[0:HD, ci4 * 512:(ci4 + 1) * 512],
                            in_=psq[:, 0:512], add=bcol))

        def vslice(kt, h):
            base = kt * NH * (HD + 1) + h * (HD + 1)
            return vaug[:, base:base + HD + 1]

        def attn_slice(h, c0):
            return attnT[(h % 2) * 64:(h % 2) * 64 + 64,
                         (h // 2) * S + c0:(h // 2) * S + c0 + CHQ]

        cfg = dict(HD=HD, NH=NH, S=S, CHQ=CHQ, qT=qT, kT=kT, qsq=qsq, cc=cc,
                   negc=negc, psA=psA, psO=psO, psR=psR, pexp=pexp, tmps=tmps,
                   recb=recb, proj_qk=proj_qk, vslice=vslice,
                   bias_slice=bias_slice, bias_ready=bias_ready,
                   bias_wait=bias_wait, bias_slice_any=relbias,
                   mask=mask_fn, extra_c=extra_c_ap,
                   scale=1.0 / math.sqrt(HD), attn_slice=attn_slice)
        emit_mha_batch(p, b, cfg)

        for eng in ("tensor", "vector", "scalar", "gpsimd", "sync"):
            @getattr(p.block, eng)
            def _(e):
                e.wait_ge(p.vC.h, p.vC.n)
        hstk.close()

        wstk = ExitStack()

        def _ws(nm, shape, dtype):
            return wstk.enter_context(
                p.nc.sbuf_tensor(f"{nm}_{p.uid()}", shape, dtype))
        xrb = [_ws(f"xrb{j}", [128, D], F32) for j in range(2)]
        xov = [_ws(f"xov{j}", [128, D], F32) for j in range(2)]
        tmpf = _ws("tmpf", [128, D], F32)
        xrsem = [p.r[9 + j] for j in range(2)]
        xrmark = [0, 0]
        stsem = [p.r[11 + j] for j in range(2)]
        psres = [wstk.enter_context(p.nc.psum_tensor(
            f"psres{b}{j}_{p.uid()}", [128, D], F32)) for j in range(2)]
        resmark = [0, 0]
        for mt in range(16):
            tile = 16 * b + mt
            pr = mt % 2
            xs = mt % 2

            @p.block.sync
            def _(e, tile=tile, xs=xs, b=b, mt=mt):
                e.wait_ge(p.vC.h, xrmark[xs])
                e.wait_ge(stsem[xs].h, stsem[xs].n)
                xrsem[xs].inc(e.dma_start(
                    out=xrb[xs][:],
                    in_=xsrc[tile * 128:(tile + 1) * 128, :]), 16)

            @p.block.tensor
            def _(e, b=b, mt=mt, pr=pr):
                e.wait_ge(p.vC.h, resmark[pr])
                ins = None
                for dt in range(DG):
                    for n0 in (0, 512):
                        ins = e.matmul(
                            psres[pr][:, n0:n0 + 512],
                            attnT[:, dt * S + mt * 128:dt * S + mt * 128 + 128],
                            wo[:, dt * 1024 + n0:dt * 1024 + n0 + 512],
                            start=(dt == 0), stop=(dt == DG - 1))
                p.tC.inc(ins)
            wm = p.tC.n

            @p.block.vector
            def _(e, tile=tile, xs=xs, pr=pr, wm=wm):
                e.wait_ge(p.tC.h, wm)
                e.wait_ge(xrsem[xs].h, xrsem[xs].n)
                p.vC.inc(e.tensor_add(out=tmpf[:], in0=psres[pr][:],
                                      in1=bo_b[:]))
                e.wait_ge(p.vC.h, p.vC.n)
                resmark[pr] = p.vC.n
                xrmark[xs] = p.vC.inc(e.tensor_add(
                    out=xov[xs][:], in0=tmpf[:], in1=xrb[xs][:]))

            @p.block.sync
            def _(e, tile=tile, xs=xs):
                e.wait_ge(p.vC.h, p.vC.n)
                stsem[xs].inc(e.dma_start(
                    out=xdst[tile * 128:(tile + 1) * 128, :],
                    in_=xov[xs][:]), 16)

        @p.block.sync
        def _(e):
            e.wait_ge(stsem[0].h, stsem[0].n)
            e.wait_ge(stsem[1].h, stsem[1].n)

        for eng in ("tensor", "vector", "scalar", "gpsimd"):
            @getattr(p.block, eng)
            def _(e):
                e.wait_ge(p.st.h, p.st.n)
                e.wait_ge(stsem[0].h, stsem[0].n)
                e.wait_ge(stsem[1].h, stsem[1].n)
        wstk.close()
    phase_end(p)



INPUT_SPECS = [
    ("L0_attn_x", (16, 2048, 1024)), ("L0_attn_Wq", (1024, 1024)),
    ("L0_attn_Wk", (1024, 1024)), ("L0_attn_Wv", (1024, 1024)),
    ("L0_attn_Wo", (1024, 1024)), ("L0_attn_quantum_phase", (16, 64)),
    ("L0_attn_quantum_amplitude", (16, 64)),
    ("L0_ffn_q_w", (1024, 1024)), ("L0_ffn_q_b", (1024,)),
    ("L0_ffn_k_w", (1024, 512)), ("L0_ffn_k_b", (512,)),
    ("L0_ffn_v_w", (1024, 512)), ("L0_ffn_v_b", (512,)),
    ("L0_ffn_o_w", (1024, 1024)), ("L0_ffn_o_b", (1024,)),
    ("L0_ffn_router_w", (1024, 8)), ("L0_ffn_router_b", (8,)),
    ("L0_ffn_w1", (8, 1024, 2048)), ("L0_ffn_b1", (8, 2048)),
    ("L0_ffn_w2", (8, 2048, 1024)), ("L0_ffn_b2", (8, 1024)),
    ("L1_attn_W_qkv", (1024, 3072)), ("L1_attn_b_qkv", (3072,)),
    ("L1_attn_W_out", (1024, 1024)), ("L1_attn_b_out", (1024,)),
    ("L1_attn_rel_bias", (2049, 16)),
    ("L1_ffn_w1", (1024, 4096)), ("L1_ffn_b1", (4096,)),
    ("L1_ffn_w2", (4096, 1024)), ("L1_ffn_b2", (1024,)),
    ("L2_attn_W_in", (1024, 2048)), ("L2_attn_conv_w", (1024, 1, 4)),
    ("L2_attn_conv_b", (1024,)), ("L2_attn_W_xproj", (1024, 33)),
    ("L2_attn_log_A", (1024, 16)), ("L2_attn_D_param", (1024,)),
    ("L2_attn_W_out", (1024, 1024)),
    ("L2_ffn_centroids", (8, 1024)), ("L2_ffn_ln_g", (8, 1024)),
    ("L2_ffn_ln_b", (8, 1024)), ("L2_ffn_w1", (8, 4096, 1024)),
    ("L2_ffn_b1", (8, 4096)), ("L2_ffn_w2", (8, 1024, 4096)),
    ("L2_ffn_b2", (8, 1024)),
    ("L3_attn_W_qkv", (1024, 3072)), ("L3_attn_b_qkv", (3072,)),
    ("L3_attn_W_out", (1024, 1024)), ("L3_attn_b_out", (1024,)),
    ("L3_ffn_ln1_g", (1024,)), ("L3_ffn_ln1_b", (1024,)),
    ("L3_ffn_ln2_g", (1024,)), ("L3_ffn_ln2_b", (1024,)),
    ("L3_ffn_Wq", (1024, 1024)), ("L3_ffn_Wk", (1024, 1024)),
    ("L3_ffn_Wv", (1024, 1024)), ("L3_ffn_Wo", (1024, 1024)),
    ("L3_ffn_Wg", (1024, 8)), ("L3_ffn_W1", (8, 1024, 4096)),
    ("L3_ffn_b1", (8, 4096)), ("L3_ffn_W2", (8, 4096, 1024)),
    ("L3_ffn_b2", (8, 1024)),
]

PHASES = [
    ("l0_ffn", "L0_ffn"), ("l0_attn", "L0_attn"), ("l0_ffn", "L0_ffn"),
    ("l1_ffn", "L1_ffn"), ("l1_attn", "L1_attn"), ("l1_ffn", "L1_ffn"),
    ("l2_ffn", "L2_ffn"), ("l2_attn", "L2_attn"), ("l2_ffn", "L2_ffn"),
    ("l3_ffn", "L3_ffn"), ("l3_attn", "L3_attn"), ("l3_ffn", "L3_ffn"),
]


class LazyAps(dict):
    def __init__(self, nc):
        super().__init__()
        self.nc = nc
        self.shapes = dict(INPUT_SPECS)

    def __missing__(self, name):
        if name == "L0_attn_x":
            ap = self.nc.dram_tensor("xin", [T, D], F32,
                                     kind="ExternalInput").ap()
        else:
            ap = self.nc.dram_tensor(name, list(self.shapes[name]), F32,
                                     kind="ExternalInput").ap()
        self[name] = ap
        return ap


def build_program(phase_filter=None):
    nc = bass.Bass("TRN2", target_bir_lowering=False)
    aps = LazyAps(nc)
    xout = nc.dram_tensor("xout", [T, D], F32, kind="ExternalOutput").ap()
    xsA = nc.dram_tensor("xscrA", [T, D], F32, kind="Internal").ap()
    xsB = nc.dram_tensor("xscrB", [T, D], F32, kind="Internal").ap()
    scratch = {"A": xsA, "B": xsB}

    phases = [(i, fn, pref) for i, (fn, pref) in enumerate(PHASES)
              if phase_filter is None or i in phase_filter]

    with ExitStack() as stack:
        block = stack.enter_context(nc.Block())
        p = P(nc, block, stack)
        emit_consts(p)
        cur = aps["L0_attn_x"]
        pingpong = ["A", "B"]

        class WView:
            def __init__(self, pref):
                self.pref = pref

            def __getitem__(self, k):
                return aps[self.pref.split("_")[0] + "_" + k]

        occs = {}
        for j, (i, fn, pref) in enumerate(phases):
            dst = xout if j == len(phases) - 1 else scratch[pingpong[j % 2]]
            W = WView(pref)
            occ = occs.get(fn, 0)
            occs[fn] = occ + 1
            if fn in ("l2_ffn", "l0_ffn", "l3_ffn"):
                globals()["phase_" + fn](p, cur, dst, W, occ=occ)
            else:
                globals()["phase_" + fn](p, cur, dst, W)
            cur = dst
    return nc, sorted(aps.keys())


FUSED = True


def kernel(**inputs):
    x = np.ascontiguousarray(np.asarray(inputs["L0_attn_x"], dtype=np.float32))
    cur = [np.ascontiguousarray(
        x[c * B_PER_CORE:(c + 1) * B_PER_CORE].reshape(T, D))
        for c in range(NCORES)]
    groups = ([set(range(len(PHASES)))] if FUSED
              else [{i} for i in range(len(PHASES))])
    for g in groups:
        nc, used = build_program(phase_filter=g)
        in_maps = []
        for core in range(NCORES):
            m = {}
            for name in used:
                if name == "L0_attn_x":
                    m["xin"] = cur[core]
                else:
                    m[name] = np.ascontiguousarray(
                        np.asarray(inputs[name], dtype=np.float32))
            in_maps.append(m)
        res = run_bass_kernel_spmd(nc, in_maps, core_ids=list(range(NCORES)))
        cur = [np.asarray(r["xout"]) for r in res.results]
    out = np.concatenate(
        [c.reshape(B_PER_CORE, S, D) for c in cur], axis=0)
    return out.astype(np.float32)
